# version 01
# baseline (speedup 1.0000x reference)
import math
from contextlib import ExitStack

import numpy as np

import concourse.bass as bass
import concourse.mybir as mybir
from concourse.bass_utils import run_bass_kernel_spmd

F32 = mybir.dt.float32
AF = mybir.ActivationFunctionType
ALU = mybir.AluOpType
AX = mybir.AxisListType

NCORE = 8
T = 16384
TLOC = T // NCORE
D = 1024
DC = D // 128
DH = 64
DF = 4096
NEXP = 8
DIN = 2048
DST = 16
DTR = 64
CHL = DIN // NCORE
NEG = -1.0e30
EPS = 1e-5
SCL = 1.0 / 32.0
EXPM_S = 6
EXPM_DEG = 8



class Prog:

    def __init__(self):
        self.nc = bass.Bass("TRN2", target_bir_lowering=False)
        self.es = ExitStack()
        self.ops = {c: [] for c in ("sy", "ge", "ve", "sc", "te")}
        self.sems = {}
        self.scount = {}

    def sem(self, key):
        if key not in self.sems:
            self.sems[key] = self.es.enter_context(self.nc.semaphore(key))
            self.scount[key] = 0
        return self.sems[key]

    def sbuf(self, name, shape, dtype=F32):
        return self.es.enter_context(self.nc.sbuf_tensor(name, list(shape), dtype))

    def psum(self, name, shape, dtype=F32):
        return self.es.enter_context(self.nc.psum_tensor(name, list(shape), dtype))

    def din(self, name, shape, dtype=F32):
        return self.nc.dram_tensor(name, list(shape), dtype, kind="ExternalInput").ap()

    def dout(self, name, shape, dtype=F32):
        return self.nc.dram_tensor(name, list(shape), dtype, kind="ExternalOutput").ap()

    def op(self, ch, fn, after=(), sig=None, amt=1):
        ev = None
        if sig is not None:
            self.sem(sig)
            self.scount[sig] += amt
            ev = (sig, self.scount[sig])
        waits = []
        for e in after:
            if e is None:
                continue
            self.sem(e[0])
            waits.append(e)
        self.ops[ch].append((fn, waits, (sig, amt) if sig else None))
        return ev

    def dma(self, ch, out, in_, after=()):
        return self.op(ch, lambda eng, o=out, i=in_: eng.dma_start(out=o, in_=i),
                       after=after, sig="d" + ch, amt=16)

    def build(self):
        nc = self.nc
        sems = self.sems
        ops = self.ops
        with nc.Block() as block:
            def mk(ch):
                def run(eng):
                    waited = {}
                    for fn, waits, sig in ops[ch]:
                        for (k, v) in waits:
                            if waited.get(k, 0) >= v:
                                continue
                            eng.wait_ge(sems[k], v)
                            waited[k] = v
                        inst = fn(eng)
                        if sig is not None:
                            k, amt = sig
                            inst.then_inc(sems[k], amt)
                return run
            block.sync(mk("sy"))
            block.gpsimd(mk("ge"))
            block.vector(mk("ve"))
            block.scalar(mk("sc"))
            block.tensor(mk("te"))
        self.es.close()
        return nc


def mm(p, psum_ap, pairs, after=(), sig=None):
    ev = None
    n = len(pairs)
    for i, (l, r) in enumerate(pairs):
        ev = p.op(
            "te",
            lambda eng, l=l, r=r, st=(i == 0), sp=(i == n - 1): eng.matmul(
                psum_ap, l, r, start=st, stop=sp),
            after=after if i == 0 else (),
            sig=sig if i == n - 1 else None)
    return ev


def tr(p, psum_ap, in_ap, ident_ap, after=(), sig=None):
    return p.op("te", lambda eng, o=psum_ap, i=in_ap, d=ident_ap:
                eng.transpose(o, i, d), after=after, sig=sig)


class LnKit:

    def __init__(self, p, tag, x2s=None):
        self.x2s = x2s if x2s is not None else p.sbuf(f"{tag}x2", (128, DC * 512))
        self.r0 = p.sbuf(f"{tag}r0", (1, 512))
        self.r1 = p.sbuf(f"{tag}r1", (1, 512))
        self.r2 = p.sbuf(f"{tag}r2", (1, 512))
        self.onescol = p.sbuf(f"{tag}oc", (128, 1))
        self.onesrow = p.sbuf(f"{tag}or", (1, 128))
        self.epsc = p.sbuf(f"{tag}ep", (1, 1))
        self.one1 = p.sbuf(f"{tag}o1", (128, 1))
        self.e_oc = p.op("ve", lambda eng: eng.memset(self.onescol[:], 1.0), sig="ve")
        self.e_or = p.op("ve", lambda eng: eng.memset(self.onesrow[:], 1.0), sig="ve")
        p.op("ve", lambda eng: eng.memset(self.epsc[:], EPS))
        p.op("ve", lambda eng: eng.memset(self.one1[:], 1.0))


def emit_ln512(p, kit, ins, outs, ps_sum, ps_b, after=(), gamma=None, beta=None):
    e_sq = None
    for c in range(DC):
        e_sq = p.op("sc", lambda eng, c=c: eng.activation(
            kit.x2s[:, c * 512:(c + 1) * 512], ins[c], AF.Square),
            after=after if c == 0 else (), sig="sc" if c == DC - 1 else None)
    e_m1 = mm(p, ps_sum[0:1, :], [(kit.onescol[:], ins[c]) for c in range(DC)],
              after=tuple(after) + (kit.e_oc,), sig="te")
    e_mu = p.op("sc", lambda eng: eng.mul(kit.r0[:], ps_sum[0:1, :], 1.0 / D),
                after=(e_m1,), sig="sc")
    e_m2 = mm(p, ps_sum[0:1, :],
              [(kit.onescol[:], kit.x2s[:, c * 512:(c + 1) * 512]) for c in range(DC)],
              after=(e_sq, e_mu), sig="te")
    e_s2 = p.op("sc", lambda eng: eng.mul(kit.r1[:], ps_sum[0:1, :], 1.0 / D),
                after=(e_m2,), sig="sc")
    e_v0 = p.op("ve", lambda eng: eng.tensor_mul(kit.r2[:], kit.r0[:], kit.r0[:]),
                after=(e_mu,), sig="ve")
    e_v1 = p.op("ve", lambda eng: eng.tensor_sub(kit.r1[:], kit.r1[:], kit.r2[:]),
                after=(e_s2, e_v0), sig="ve")
    e_l = p.op("sc", lambda eng: eng.activation(kit.r2[:], kit.r1[:], AF.Ln,
                                                bias=kit.epsc[0:1, 0:1]),
               after=(e_v1,))
    e_r = p.op("sc", lambda eng: eng.activation(kit.r2[:], kit.r2[:], AF.Exp,
                                                scale=-0.5), sig="sc")
    e_b1 = mm(p, ps_b[:, :], [(kit.onesrow[:], kit.r0[:])],
              after=(e_mu, kit.e_or), sig="te")
    e_sub = None
    for c in range(DC):
        e_sub = p.op("ve", lambda eng, c=c: eng.tensor_sub(outs[c], ins[c], ps_b[:, :]),
                     after=(e_b1,) if c == 0 else (),
                     sig="ve" if c == DC - 1 else None)
    e_b2 = mm(p, ps_b[:, :], [(kit.onesrow[:], kit.r2[:])],
              after=(e_r, e_sub), sig="te")
    e_fin = None
    for c in range(DC):
        e_fin = p.op("ve", lambda eng, c=c: eng.tensor_mul(outs[c], outs[c], ps_b[:, :]),
                     after=(e_b2,) if c == 0 else (),
                     sig="ve" if (gamma is None and c == DC - 1) else None)
    if gamma is not None:
        for c in range(DC):
            e_fin = p.op("ve", lambda eng, c=c: eng.tensor_scalar(
                outs[c], outs[c], gamma[c], beta[c], ALU.mult, ALU.add),
                sig="ve" if c == DC - 1 else None)
    return e_fin


def emit_top2(p, s, tile, wout, mode, after=()):
    prev = [None]

    def v(fn, extra=()):
        prev[0] = p.op("ve", fn, after=(prev[0],) + tuple(extra), sig="ve")
        return prev[0]

    v(lambda eng: eng.reduce_max(s["m1"], tile, AX.X), extra=after)
    v(lambda eng: eng.tensor_scalar(s["eq"], tile, s["m1"], None, ALU.is_equal))
    v(lambda eng: eng.tensor_tensor_scan(s["cs"], s["ones8"], s["eq"], 0.0,
                                         ALU.mult, ALU.add))
    v(lambda eng: eng.tensor_scalar(s["cs"], s["cs"], 1.0, None, ALU.is_equal))
    v(lambda eng: eng.tensor_mul(s["sel1"], s["eq"], s["cs"]))
    v(lambda eng: eng.scalar_tensor_tensor(s["l2"], s["sel1"], NEG, tile,
                                           ALU.mult, ALU.add))
    v(lambda eng: eng.reduce_max(s["m2"], s["l2"], AX.X))
    v(lambda eng: eng.tensor_scalar(s["eq"], s["l2"], s["m2"], None, ALU.is_equal))
    v(lambda eng: eng.tensor_tensor_scan(s["cs"], s["ones8"], s["eq"], 0.0,
                                         ALU.mult, ALU.add))
    v(lambda eng: eng.tensor_scalar(s["cs"], s["cs"], 1.0, None, ALU.is_equal))
    e_sel2 = v(lambda eng: eng.tensor_mul(s["sel2"], s["eq"], s["cs"]))
    if mode == "softmax":
        v(lambda eng: eng.tensor_sub(s["t1"], s["m2"], s["m1"]))
        e_e2 = p.op("sc", lambda eng: eng.activation(s["t1"], s["t1"], AF.Exp),
                    after=(prev[0],), sig="sc")
        e_z = p.op("sc", lambda eng: eng.activation(
            s["m2"], s["t1"], AF.Identity, bias=s["one1"]), sig="sc")
        v(lambda eng: eng.reciprocal(s["m1"], s["m2"]), extra=(e_z,))
        v(lambda eng: eng.tensor_mul(s["m2"], s["t1"], s["m1"]))
        v(lambda eng: eng.tensor_scalar(wout, s["sel1"], s["m1"], None, ALU.mult))
        return v(lambda eng: eng.scalar_tensor_tensor(wout, s["sel2"], s["m2"], wout,
                                                      ALU.mult, ALU.add))
    else:
        v(lambda eng: eng.tensor_scalar(wout, s["sel1"], s["m1"], None, ALU.mult))
        return v(lambda eng: eng.scalar_tensor_tensor(wout, s["sel2"], s["m2"], wout,
                                                      ALU.mult, ALU.add))


def top2_scratch(p, tag):
    mk8 = lambda n: p.sbuf(f"{tag}{n}", (128, 8))[:]
    mk1 = lambda n: p.sbuf(f"{tag}{n}", (128, 1))[:]
    s = dict(ones8=mk8("o8"), eq=mk8("eq"), cs=mk8("cs"), sel1=mk8("s1"),
             sel2=mk8("s2"), l2=mk8("l2"), m1=mk1("m1"), m2=mk1("m2"), t1=mk1("t1"),
             one1=mk1("o1"))
    s["e_ones"] = p.op("ve", lambda eng: eng.memset(s["ones8"], 1.0), sig="ve")
    p.op("ve", lambda eng: eng.memset(s["one1"], 1.0))
    return s



def build_A(dbg=False, dbg2=False):
    p = Prog()
    xT = p.din("xT", (D, T))
    xownH = p.din("xownT", (D, TLOC))
    wqT = p.din("wqT", (D, DH))
    wkT = p.din("wkT", (D, DH))
    wvT = p.din("wvT", (D, DH))
    woT = p.din("woT", (DH, D))
    bq = p.din("bq", (DH, 1))
    bk = p.din("bk", (DH, 1))
    bv = p.din("bv", (DH, 1))
    bo = p.din("bo", (DC, 128, 1))
    qg = p.din("qg", (128, 16))
    qgkb = p.din("qgkb", (128, 16 * 32))
    iotaf = p.din("iotaf", (128, 512))
    eye = p.din("eye128", (128, 128))
    x1T = p.dout("x1T", (D, TLOC))

    ident = p.sbuf("ident", (128, 128))
    s_wq = p.sbuf("s_wq", (128, DC * DH))
    s_wk = p.sbuf("s_wk", (128, DC * DH))
    s_wv = p.sbuf("s_wv", (128, DC * DH))
    s_wo = p.sbuf("s_wo", (DH, D))
    s_bq = p.sbuf("s_bq", (DH, 1))
    s_bk = p.sbuf("s_bk", (DH, 1))
    s_bv = p.sbuf("s_bv", (DH, 1))
    s_bo = p.sbuf("s_bo", (128, DC))
    s_qg = p.sbuf("s_qg", (128, 16))
    s_qgkb = p.sbuf("s_qgkb", (128, 16 * 32))
    s_iota = p.sbuf("s_iota", (128, 512))
    xst0 = p.sbuf("xst0", (128, DC * 512))
    xst = [xst0, xst0]
    kT = p.sbuf("kT", (DH, T))
    qTt = p.sbuf("qTt", (DH, TLOC))
    vst = p.sbuf("vst", (DH, 512))
    vtok = p.sbuf("vtok", (128, (T // 128) * DH))
    erow = p.sbuf("erow", (128, T // 4))
    etile = [p.sbuf(f"etile{i}", (128, 128)) for i in range(2)]
    m01 = p.sbuf("m01", (128, 512))
    t128 = p.sbuf("t128", (128, 6))
    rrow = p.sbuf("rrow", (1, 128))
    avq = p.sbuf("avq", (DH, 128))
    avall = p.sbuf("avall", (DH, TLOC))
    attnb = p.sbuf("attnb", (128, 512))
    xres = p.sbuf("xres", (128, DC * 512))
    kit = LnKit(p, "lnA", x2s=xst[0])

    ps_kv = p.psum("ps_kv", (DH, 512))
    ps_sc = [p.psum(f"ps_sc{i}", (128, 512)) for i in range(2)]
    ps_tr = p.psum("ps_tr", (128, 256))
    ps_av = p.psum("ps_av", (DH, 128))
    ps_b = p.psum("ps_b", (128, 512))
    ps_sum = p.psum("ps_sum", (1, 512))

    e_id = p.dma("sy", ident[:], eye)
    for (dst, src) in ((s_wq, wqT), (s_wk, wkT), (s_wv, wvT)):
        for c in range(DC):
            p.dma("sy", dst[:, c * DH:(c + 1) * DH], src[c * 128:(c + 1) * 128, :])
    p.dma("sy", s_wo[:], woT)
    p.dma("sy", s_bq[:], bq)
    p.dma("sy", s_bk[:], bk)
    p.dma("sy", s_bv[:], bv)
    for c in range(DC):
        p.dma("sy", s_bo[:, c:c + 1], bo[c])
    p.dma("sy", s_qg[:], qg)
    p.dma("sy", s_qgkb[:], qgkb)
    e_wload = p.dma("sy", s_iota[:], iotaf)

    NKB = T // 512
    e_stage_free = None
    e_kv_bank_free = None
    e_vst_free = None
    e_vtok = None
    e_kevac = None
    for kb in range(NKB):
        buf = 0
        e_ld = None
        for c in range(DC):
            e_ld = p.dma("sy", xst[buf][:, c * 512:(c + 1) * 512],
                         xT[c * 128:(c + 1) * 128, kb * 512:(kb + 1) * 512],
                         after=(e_stage_free,) if c == 0 else ())
        pairs_k = [(s_wk[:, c * DH:(c + 1) * DH], xst[buf][:, c * 512:(c + 1) * 512])
                   for c in range(DC)]
        e_mmk = mm(p, ps_kv[:], pairs_k, after=(e_ld, e_wload, e_kv_bank_free),
                   sig="te")
        e_kevac = p.op("sc", lambda eng, kb=kb: eng.activation(
            kT[:, kb * 512:(kb + 1) * 512], ps_kv[:], AF.Identity,
            bias=s_bk[:, 0:1]), after=(e_mmk,), sig="sc")
        pairs_v = [(s_wv[:, c * DH:(c + 1) * DH], xst[buf][:, c * 512:(c + 1) * 512])
                   for c in range(DC)]
        e_mmv = mm(p, ps_kv[:], pairs_v, after=(e_kevac, e_vst_free), sig="te")
        e_vevac = p.op("sc", lambda eng: eng.activation(
            vst[:], ps_kv[:], AF.Identity, bias=s_bv[:, 0:1]),
            after=(e_mmv,), sig="sc")
        e_trv = None
        for j in range(4):
            e_trv = tr(p, ps_tr[:, j * DH:(j + 1) * DH],
                       vst[0:DH, j * 128:(j + 1) * 128], ident[0:DH, 0:DH],
                       after=(e_vevac, e_id, e_vtok) if j == 0 else (),
                       sig="te" if j == 3 else None)
        e_vtok = p.op("sc", lambda eng, kb=kb: eng.copy(
            vtok[:, kb * 4 * DH:(kb + 1) * 4 * DH], ps_tr[:, :]),
            after=(e_trv,), sig="sc")
        e_vst_free = e_trv
        e_kv_bank_free = e_vevac
        e_stage_free = e_mmv
    e_vtok_all = e_vtok

    e_q = None
    e_qstage_free = None
    for tb in range(TLOC // 512):
        e_ld = None
        for c in range(DC):
            e_ld = p.dma("sy", xres[:, c * 512:(c + 1) * 512],
                         xownH[c * 128:(c + 1) * 128, tb * 512:(tb + 1) * 512],
                         after=(e_qstage_free,) if c == 0 else ())
        pairs_q = [(s_wq[:, c * DH:(c + 1) * DH], xres[:, c * 512:(c + 1) * 512])
                   for c in range(DC)]
        e_mmq = mm(p, ps_kv[:], pairs_q, after=(e_ld, e_kv_bank_free), sig="te")
        e_kv_bank_free = None
        e_q = p.op("sc", lambda eng, tb=tb: eng.activation(
            qTt[:, tb * 512:(tb + 1) * 512], ps_kv[:], AF.Identity,
            bias=s_bq[:, 0:1]), after=(e_mmq,), sig="sc")
        e_qstage_free = e_mmq
    e_q_all = e_q

    NHB = (T // 4) // 512
    e_avq = None
    e_evac_prev = [None, None]
    e_avmm_prev = None
    e_avused_prev = None
    for qt in range(16):
        qcol = s_qg[:, qt:qt + 1]
        e_ini = p.op("ve", lambda eng: eng.memset(t128[:, 2:3], -3.0e38))
        e_ini = p.op("ve", lambda eng: eng.memset(t128[:, 3:4], 0.0), sig="ve")
        e_avz = p.op("ve", lambda eng: eng.memset(avq[:], 0.0),
                     after=(e_avq,), sig="ve")
        for half in range(4):
            e_evac = None
            for kb in range(NHB):
                b = kb % 2
                kbg = half * NHB + kb
                e_mms = mm(p, ps_sc[b][:],
                           [(qTt[0:DH, qt * 128:(qt + 1) * 128],
                             kT[0:DH, kbg * 512:(kbg + 1) * 512])],
                           after=(e_q_all, e_kevac, e_evac_prev[b]), sig="te")
                p.op("ve", lambda eng, qt=qt, kbg=kbg: eng.tensor_scalar(
                    m01[:], s_iota[:], s_qgkb[:, qt * 32 + kbg: qt * 32 + kbg + 1],
                    None, ALU.is_gt), after=(e_wload,))
                e_evac = p.op("ve", lambda eng, kb=kb, b=b: eng.scalar_tensor_tensor(
                    erow[:, kb * 512:(kb + 1) * 512], m01[:], NEG, ps_sc[b][:],
                    ALU.mult, ALU.add),
                    after=(e_mms, e_avmm_prev) if kb == 0 else (e_mms,), sig="ve")
                e_evac_prev[b] = e_evac
                if dbg2 == 2 and qt == 0 and half == 0 and kb == 1:
                    d_m01 = p.dout("d_m01", (128, 512))
                    d_t0 = p.dout("d_t0", (128, 1))
                    d_eseg = p.dout("d_eseg", (128, 1024))
                    p.dma("ge", d_m01, m01[:], after=(e_evac,))
                    p.dma("ge", d_t0, t128[:, 0:1], after=(e_evac,))
                    e_dbg3 = p.dma("ge", d_eseg, erow[:, 0:1024], after=(e_evac,))
                    p.op("ve", lambda eng: eng.engine_nop(), after=(e_dbg3,))
                    return p

            e_c0 = p.op("ve", lambda eng: eng.reduce_max(
                t128[:, 0:1], erow[:], AX.X), sig="ve")
            e_c1 = p.op("ve", lambda eng: eng.tensor_sub(
                t128[:, 4:5], t128[:, 2:3], t128[:, 0:1]), after=(e_c0,), sig="ve")
            e_c2 = p.op("ve", lambda eng: eng.tensor_scalar_min(
                t128[:, 4:5], t128[:, 4:5], 0.0), after=(e_c1,), sig="ve")
            e_c3 = p.op("ve", lambda eng: eng.tensor_max(
                t128[:, 2:3], t128[:, 2:3], t128[:, 0:1]), after=(e_c2,), sig="ve")
            e_bias = p.op("ve", lambda eng: eng.tensor_scalar(
                t128[:, 1:2], t128[:, 2:3], -SCL, None, ALU.mult),
                after=(e_c3,), sig="ve")
            e_alpha = p.op("sc", lambda eng: eng.activation(
                t128[:, 4:5], t128[:, 4:5], AF.Exp, scale=SCL),
                after=(e_bias,), sig="sc")
            e_exp = p.op("sc", lambda eng: eng.activation(
                erow[:], erow[:], AF.Exp, bias=t128[:, 1:2], scale=SCL,
                accum_out=t128[:, 5:6]), sig="sc")
            e_l = p.op("ve", lambda eng: eng.scalar_tensor_tensor(
                t128[:, 3:4], t128[:, 3:4], t128[:, 4:5], t128[:, 5:6],
                ALU.mult, ALU.add), after=(e_exp,), sig="ve")
            e_tra = tr(p, ps_sum[0:1, 0:128], t128[:, 4:5], ident[:],
                       after=(e_alpha, e_avused_prev), sig="te")
            e_rr = p.op("sc", lambda eng: eng.copy(rrow[:], ps_sum[0:1, 0:128]),
                        after=(e_tra,), sig="sc")
            e_ab = mm(p, ps_b[0:DH, 0:128], [(kit.onesrow[0:1, 0:DH], rrow[:])],
                      after=(e_rr, kit.e_or), sig="te")
            e_avs = p.op("ve", lambda eng: eng.tensor_mul(
                avq[:], avq[:], ps_b[0:DH, 0:128]), after=(e_ab, e_avz), sig="ve")
            e_slot = [None, None]
            e_avmm = None
            for kt in range(NHB * 4):
                b = kt % 2
                ktg = half * NHB * 4 + kt
                e_trE = tr(p, ps_tr[:, b * 128:(b + 1) * 128],
                           erow[:, kt * 128:(kt + 1) * 128], ident[:],
                           after=(e_exp, e_slot[b]), sig="te")
                e_ev = p.op("sc", lambda eng, b=b: eng.copy(
                    etile[b][:], ps_tr[:, b * 128:(b + 1) * 128]),
                    after=(e_trE,), sig="sc")
                e_avmm = p.op("te", lambda eng, ktg=ktg, b=b, st=(kt == 0),
                              sp=(kt == NHB * 4 - 1): eng.matmul(
                    ps_av[:], vtok[:, ktg * DH:(ktg + 1) * DH], etile[b][:],
                    start=st, stop=sp),
                    after=(e_ev, e_vtok_all, e_avs) if kt == 0 else (e_ev,),
                    sig="te" if kt == NHB * 4 - 1 else None)
                e_slot[b] = e_avmm
            e_avmm_prev = e_avmm
            e_avused_prev = p.op("ve", lambda eng: eng.tensor_add(
                avq[:], avq[:], ps_av[:]), after=(e_avmm, e_avs), sig="ve")
            if dbg2 and qt == 0 and half == 0:
                d_er = p.dout("d_erow", (128, T // 4))
                d_avq = p.dout("d_avq", (DH, 128))
                d_t = p.dout("d_t128", (128, 6))
                p.dma("ge", d_er, erow[:], after=(e_exp,))
                p.dma("ge", d_avq, avq[:], after=(e_avused_prev,))
                e_dbg2 = p.dma("ge", d_t, t128[:], after=(e_l,))
                p.op("ve", lambda eng: eng.engine_nop(), after=(e_dbg2,))
                return p

        p.op("ve", lambda eng: eng.reciprocal(t128[:, 4:5], t128[:, 3:4]),
             after=(e_l,))
        e_f = p.op("ve", lambda eng: eng.memset(t128[:, 5:6], 0.0), sig="ve")
        e_tr2 = tr(p, ps_sum[0:1, 0:128], t128[:, 4:5], ident[:],
                   after=(e_f,), sig="te")
        e_rr2 = p.op("sc", lambda eng: eng.copy(rrow[:], ps_sum[0:1, 0:128]),
                     after=(e_tr2,), sig="sc")
        e_ab2 = mm(p, ps_b[0:DH, 0:128], [(kit.onesrow[0:1, 0:DH], rrow[:])],
                   after=(e_rr2,), sig="te")
        e_avq = p.op("ve", lambda eng, qt=qt: eng.tensor_tensor(
            avall[0:DH, qt * 128:(qt + 1) * 128], avq[:], ps_b[0:DH, 0:128],
            ALU.mult), after=(e_ab2, e_avused_prev), sig="ve")
    e_av_all = e_avq

    if dbg:
        d_k = p.dout("d_kT", (DH, T))
        d_q = p.dout("d_qT", (DH, TLOC))
        d_v = p.dout("d_vtok", (128, (T // 128) * DH))
        d_av = p.dout("d_avall", (DH, TLOC))
        p.dma("ge", d_k, kT[:], after=(e_kevac,))
        p.dma("ge", d_q, qTt[:], after=(e_q_all,))
        p.dma("ge", d_v, vtok[:], after=(e_vtok_all,))
        e_dbg = p.dma("ge", d_av, avall[:], after=(e_av_all,))

    e_res_free = None
    for tb in range(TLOC // 512):
        e_ld = None
        for c in range(DC):
            e_ld = p.dma("ge", xres[:, c * 512:(c + 1) * 512],
                         xownH[c * 128:(c + 1) * 128, tb * 512:(tb + 1) * 512],
                         after=(e_res_free, e_q_all) if c == 0 else ())
        e_add = None
        for c in range(DC):
            e_mmo = mm(p, ps_sc[0][:],
                       [(s_wo[0:DH, c * 128:(c + 1) * 128],
                         avall[0:DH, tb * 512:(tb + 1) * 512])],
                       after=(e_av_all, e_add, e_evac_prev[0]), sig="te")
            e_att = p.op("sc", lambda eng, c=c: eng.activation(
                attnb[:], ps_sc[0][:], AF.Identity, bias=s_bo[:, c:c + 1]),
                after=(e_mmo,), sig="sc")
            e_add = p.op("ve", lambda eng, c=c: eng.tensor_add(
                xres[:, c * 512:(c + 1) * 512], xres[:, c * 512:(c + 1) * 512],
                attnb[:]), after=(e_att, e_ld), sig="ve")
        chunks = [xres[:, c * 512:(c + 1) * 512] for c in range(DC)]
        e_ln = emit_ln512(p, kit, chunks, chunks, ps_sum, ps_b, after=(e_add,))
        e_st = None
        for c in range(DC):
            e_st = p.dma("sy", x1T[c * 128:(c + 1) * 128, tb * 512:(tb + 1) * 512],
                         xres[:, c * 512:(c + 1) * 512],
                         after=(e_ln,) if c == 0 else ())
        e_res_free = e_st
    p.op("ve", lambda eng: eng.engine_nop(), after=(e_st,))
    return p



def build_B(dbg=False):
    p = Prog()
    x1T = p.din("x1T", (D, TLOC))
    lng = p.din("lng", (DC, 128, 1))
    lnb = p.din("lnb", (DC, 128, 1))
    abelW = p.din("abelW", (DC, 128, DC, 128))
    abelB = p.din("abelB", (DC, 128, 1))
    gateW = p.din("gateW", (D, 8))
    gateBB = p.din("gateBB", (128, 8))
    w1t = p.din("w1t", (NEXP, 8, 128, DC, 512))
    b1r = p.din("b1r", (NEXP, 128, 32))
    w2t = p.din("w2t", (NEXP, 8, 128, 4, DC, 128))
    b2t = p.din("b2t", (128, NEXP * DC))
    gens = p.din("gens", (3, D, D))
    gensT = p.din("gensT", (3, D, D))
    eye = p.din("eye", (D, D))
    eye128 = p.din("eye128", (128, 128))
    mixw = p.din("mixw", (2 * D, 1))
    mixb = p.din("mixb", (1, 1))
    x2T = p.dout("x2T", (D, TLOC))
    TH = TLOC // 2

    ident = p.sbuf("ident", (128, 128))
    s_lng = p.sbuf("s_lng", (128, DC))
    s_lnb = p.sbuf("s_lnb", (128, DC))
    s_abelB = p.sbuf("s_abelB", (128, DC))
    s_gw = p.sbuf("s_gw", (128, DC * 8))
    s_gbb = p.sbuf("s_gbb", (128, 8))
    s_mixw = p.sbuf("s_mixw", (128, 16))
    s_mixb = p.sbuf("s_mixb", (1, 1))
    s_b1 = p.sbuf("s_b1", (128, 32))
    s_b2 = p.sbuf("s_b2", (128, NEXP * DC))
    abw = p.sbuf("abw", (128, DC * 128))
    mxX = p.sbuf("mxX", (128, DC * D))
    mxR = p.sbuf("mxR", (128, DC * D))
    mxT = p.sbuf("mxT", (128, DC * D))
    mxrot = p.sbuf("mxrot", (128, DC * D))
    eyebuf = p.sbuf("eyebuf", (128, D))
    xn = mxrot
    moe = mxR
    abelT = mxX
    w1b = p.sbuf("w1b", (128, DC * 512))
    w2b = p.sbuf("w2b", (128, 4 * DC * 128))
    resb = w2b
    hgr = p.sbuf("hgr", (128, 4 * 512))
    lgt = p.sbuf("lgt", (128, 8))
    wd = p.sbuf("wd", (128, 16 * 8))
    wrow = p.sbuf("wrow", (1, 512))
    grow = p.sbuf("grow", (1, TLOC))
    gb = p.sbuf("gbuf", (128, 512))
    t2 = top2_scratch(p, "t2B")
    kit = LnKit(p, "lnB", x2s=w1b)

    ps_a = p.psum("ps_a", (128, 512))
    ps_h = [p.psum(f"ps_h{i}", (128, 512)) for i in range(2)]
    ps_eo = [p.psum(f"ps_eo{i}", (128, 512)) for i in range(2)]
    ps_bc = p.psum("ps_bc", (128, 512))
    ps_sum = p.psum("ps_sum", (1, 512))

    e_id = p.dma("sy", ident[:], eye128)
    for c in range(DC):
        p.dma("sy", s_lng[:, c:c + 1], lng[c])
        p.dma("sy", s_lnb[:, c:c + 1], lnb[c])
        p.dma("sy", s_abelB[:, c:c + 1], abelB[c])
        p.dma("sy", s_gw[:, c * 8:(c + 1) * 8], gateW[c * 128:(c + 1) * 128, :])
    p.dma("sy", s_gbb[:], gateBB)
    for c in range(16):
        p.dma("sy", s_mixw[:, c:c + 1], mixw[c * 128:(c + 1) * 128, :])
    p.dma("sy", s_mixb[:], mixb)
    p.dma("sy", s_b2[:], b2t)
    e_const = p.dma("sy", eyebuf[:], eye[0:128, :])

    def mm1024(dst, lhsT_buf, rhs_buf, after=(), scale=1.0, add_eye=False):
        e_last = None
        for r in range(DC):
            for half in range(2):
                pairs = [(lhsT_buf[:, c * D + r * 128: c * D + (r + 1) * 128],
                          rhs_buf[:, c * D + half * 512: c * D + half * 512 + 512])
                         for c in range(DC)]
                e_g = mm(p, ps_a[:], pairs,
                         after=tuple(after) + ((e_last,) if e_last else ()),
                         sig="te")
                if add_eye:
                    e_ey = p.dma("ge", eyebuf[:, half * 512:half * 512 + 512],
                                 eye[r * 128:(r + 1) * 128, half * 512:half * 512 + 512],
                                 after=(e_last,) if e_last else ())
                    e_c = p.op("sc", lambda eng, r=r, half=half, scale=scale:
                               eng.mul(dst[:, r * D + half * 512:
                                           r * D + half * 512 + 512],
                                       ps_a[:], scale), after=(e_g,), sig="sc")
                    e_last = p.op("ve", lambda eng, r=r, half=half: eng.tensor_add(
                        dst[:, r * D + half * 512: r * D + half * 512 + 512],
                        dst[:, r * D + half * 512: r * D + half * 512 + 512],
                        eyebuf[:, half * 512: half * 512 + 512]),
                        after=(e_c, e_ey), sig="ve")
                else:
                    e_last = p.op("sc", lambda eng, r=r, half=half, scale=scale:
                                  eng.mul(dst[:, r * D + half * 512:
                                              r * D + half * 512 + 512],
                                          ps_a[:], scale), after=(e_g,), sig="sc")
        return e_last

    def transpose1024(dst, srcb, after=()):
        e_last = None
        for r in range(DC):
            for c in range(DC):
                e_t = tr(p, ps_a[:, 0:128],
                         srcb[:, c * D + r * 128:c * D + (r + 1) * 128],
                         ident[:],
                         after=tuple(after) + ((e_last,) if e_last else (e_id,)),
                         sig="te")
                e_last = p.op("sc", lambda eng, r=r, c=c: eng.copy(
                    dst[:, r * D + c * 128: r * D + (c + 1) * 128],
                    ps_a[:, 0:128]), after=(e_t,), sig="sc")
        return e_last

    DEG = 20
    e_rot = None
    e_rotL = None
    for gi in (2, 1, 0):
        e_x = None
        for r in range(DC):
            e_x = p.dma("sy", mxX[:, r * D:(r + 1) * D],
                        gens[gi, r * 128:(r + 1) * 128, :],
                        after=(e_rot,) if (gi != 2 and r == 0) else ())
        e_r = e_rot
        for r in range(DC):
            e_l = p.dma("sy", mxR[:, r * D:(r + 1) * D],
                        gensT[gi, r * 128:(r + 1) * 128, :],
                        after=(e_r,) if e_r else ())
            e_s = p.op("sc", lambda eng, r=r: eng.mul(
                mxR[:, r * D:(r + 1) * D], mxR[:, r * D:(r + 1) * D],
                1.0 / DEG), after=(e_l,), sig="sc")
            e_ey = p.dma("ge", eyebuf[:], eye[r * 128:(r + 1) * 128, :],
                         after=(e_r,) if e_r else ())
            e_r = p.op("ve", lambda eng, r=r: eng.tensor_add(
                mxR[:, r * D:(r + 1) * D], mxR[:, r * D:(r + 1) * D],
                eyebuf[:]), after=(e_s, e_ey), sig="ve")
        cur, oth = mxR, mxT
        for k in range(DEG - 1, 0, -1):
            e_r = mm1024(oth, mxX, cur, after=(e_r, e_x), scale=1.0 / k,
                         add_eye=True)
            cur, oth = oth, cur
        assert cur is mxT
        if gi == 2:
            e_rot = p.op("ve", lambda eng: eng.tensor_copy(mxrot[:], mxT[:]),
                         after=(e_r,), sig="ve")
        else:
            e_tp = transpose1024(mxX, mxrot, after=(e_r,))
            e_rot = mm1024(mxrot, mxX, mxT, after=(e_tp,))
    e_rotL = transpose1024(mxT, mxrot, after=(e_rot,))

    e_x2st = None
    kbank_free = {0: None, 1: None}
    for th in range(2):
        toff = th * TH
        e_lx = None
        for c in range(DC):
            e_lx = p.dma("sy", moe[:, c * TH:(c + 1) * TH],
                         x1T[c * 128:(c + 1) * 128, toff:toff + TH],
                         after=(e_x2st, e_rotL) if c == 0 else ())
        e_ln = None
        for blk in range(TH // 512):
            ins = [moe[:, c * TH + blk * 512: c * TH + blk * 512 + 512]
                   for c in range(DC)]
            outs = [xn[:, c * TH + blk * 512: c * TH + blk * 512 + 512]
                    for c in range(DC)]
            e_ln = emit_ln512(p, kit, ins, outs, ps_sum, ps_bc,
                              after=(e_lx, e_ln, e_rotL),
                              gamma=[s_lng[:, c:c + 1] for c in range(DC)],
                              beta=[s_lnb[:, c:c + 1] for c in range(DC)])
        e_wdT = None
        for tt in range(TH // 128):
            e_gl = mm(p, ps_a[:, 0:8],
                      [(xn[:, c * TH + tt * 128: c * TH + (tt + 1) * 128],
                        s_gw[:, c * 8:(c + 1) * 8]) for c in range(DC)],
                      after=(e_ln, e_wdT), sig="te")
            e_lg = p.op("ve", lambda eng: eng.tensor_add(
                lgt[:], ps_a[0:128, 0:8], s_gbb[:]), after=(e_gl,), sig="ve")
            e_wdT = emit_top2(p, t2, lgt[:], wd[:, tt * 8:(tt + 1) * 8], "softmax",
                              after=(e_lg, t2["e_ones"]))
        e_ab = None
        for fc in range(DC):
            e_lw = p.dma("ge", abw[:], abelW[fc].rearrange("p c f -> p (c f)"),
                         after=(e_ab,) if e_ab else ())
            for tb in range(TH // 512):
                e_g = mm(p, ps_h[tb % 2][:],
                         [(abw[:, c * 128:(c + 1) * 128],
                           xn[:, c * TH + tb * 512: c * TH + tb * 512 + 512])
                          for c in range(DC)],
                         after=(e_lw, e_ln, kbank_free[tb % 2]), sig="te")
                e_ab = p.op("sc", lambda eng, fc=fc, tb=tb: eng.activation(
                    abelT[:, fc * TH + tb * 512: fc * TH + tb * 512 + 512],
                    ps_h[tb % 2][:], AF.Gelu, bias=s_abelB[:, fc:fc + 1]),
                    after=(e_g,), sig="sc")
                kbank_free[tb % 2] = e_ab
        e_minit = p.op("ve", lambda eng: eng.memset(moe[:], 0.0),
                       after=(e_ln, e_rotL, e_ab), sig="ve")
        e_h_free = e_ab
        e_w1 = e_w2 = None
        e_w1use = None
        e_w2use = e_x2st
        for e in range(NEXP):
            e_b1l = p.dma("sy", s_b1[:], b1r[e],
                          after=(e_h_free,) if e_h_free else ())
            for fg in range(8):
                e_w1 = p.dma("sy", w1b[:], w1t[e, fg].rearrange("p c f -> p (c f)"),
                             after=(e_w1use,) if e_w1use else (e_ln, e_ab))
                e_w2 = p.dma("sy", w2b[:], w2t[e, fg].rearrange("p a b f -> p (a b f)"),
                             after=(e_w2use,) if e_w2use else ())
                for tb in range(TH // 512):
                    e_tp = None
                    for q4 in range(4):
                        tt = tb * 4 + q4
                        e_tp = tr(p, ps_sum[0:1, q4 * 128:(q4 + 1) * 128],
                                  wd[:, tt * 8 + e: tt * 8 + e + 1], ident[:],
                                  after=(e_wdT, e_h_free) if q4 == 0 else (),
                                  sig="te" if q4 == 3 else None)
                    e_wr = p.op("sc", lambda eng: eng.copy(
                        wrow[0:1, :], ps_sum[0:1, :]), after=(e_tp,), sig="sc")
                    e_bc = mm(p, ps_bc[:], [(kit.onesrow[:], wrow[0:1, :])],
                              after=(e_wr,), sig="te")
                    e_hg = []
                    for f4 in range(4):
                        e_hm = mm(p, ps_h[f4 % 2][:],
                                  [(w1b[:, c * 512 + f4 * 128: c * 512 + (f4 + 1) * 128],
                                    xn[:, c * TH + tb * 512: c * TH + tb * 512 + 512])
                                   for c in range(DC)],
                                  after=(e_w1, e_b1l, e_ln,
                                         e_hg[-2] if len(e_hg) >= 2 else e_h_free),
                                  sig="te")
                        e_w1use = e_hm
                        e_he = p.op("sc", lambda eng, e=e, fg=fg, f4=f4:
                                    eng.activation(hgr[:, f4 * 512:(f4 + 1) * 512],
                                                   ps_h[f4 % 2][:], AF.Gelu,
                                                   bias=s_b1[:, fg * 4 + f4:
                                                             fg * 4 + f4 + 1]),
                                    after=(e_hm,), sig="sc")
                        e_hw = p.op("ve", lambda eng, f4=f4: eng.tensor_mul(
                            hgr[:, f4 * 512:(f4 + 1) * 512],
                            hgr[:, f4 * 512:(f4 + 1) * 512], ps_bc[:]),
                            after=(e_he, e_bc), sig="ve")
                        e_hg.append(e_hw)
                    e_moe = None
                    for dt in range(DC):
                        e_em = mm(p, ps_eo[dt % 2][:],
                                  [(w2b[:, (f4 * DC + dt) * 128:
                                        (f4 * DC + dt + 1) * 128],
                                    hgr[:, f4 * 512:(f4 + 1) * 512])
                                   for f4 in range(4)],
                                  after=(e_w2, e_hg[-1], e_minit, e_moe), sig="te")
                        e_w2use = e_em
                        e_moe = p.op("ve", lambda eng, dt=dt, tb=tb: eng.tensor_add(
                            moe[:, dt * TH + tb * 512: dt * TH + tb * 512 + 512],
                            moe[:, dt * TH + tb * 512: dt * TH + tb * 512 + 512],
                            ps_eo[dt % 2][:]), after=(e_em,), sig="ve")
                    e_h_free = e_moe
        if dbg and th == 0:
            d_xn = p.dout("d_xn", (128, DC * TH))
            d_ab = p.dout("d_ab", (128, DC * TH))
            d_wd = p.dout("d_wd", (128, 128))
            d_moe = p.dout("d_moe", (128, DC * TH))
            d_rot = p.dout("d_rot", (128, DC * D))
            p.dma("ge", d_xn, xn[:], after=(e_ln,))
            p.dma("ge", d_ab, abelT[:], after=(e_ab,))
            p.dma("ge", d_wd, wd[:], after=(e_wdT,))
            p.dma("ge", d_moe, moe[:], after=(e_h_free,))
            p.dma("ge", d_rot, mxT[:], after=(e_rotL,))
        e_na = None
        for j in range(DC):
            for tb in range(TH // 512):
                e_g = mm(p, ps_a[:],
                         [(mxT[:, c * D + j * 128: c * D + (j + 1) * 128],
                           moe[:, c * TH + tb * 512: c * TH + tb * 512 + 512])
                          for c in range(DC)],
                         after=(e_rotL, e_h_free, e_na), sig="te")
                e_na = p.op("sc", lambda eng, j=j, tb=tb: eng.copy(
                    xn[:, j * TH + tb * 512: j * TH + tb * 512 + 512], ps_a[:]),
                    after=(e_g,), sig="sc")
        if dbg and th == 0:
            d_na = p.dout("d_na", (128, DC * TH))
            p.dma("ge", d_na, xn[:], after=(e_na,))
        e_gr = None
        for tb in range(TH // 512):
            pairs = ([(s_mixw[:, c:c + 1],
                       abelT[:, c * TH + tb * 512: c * TH + tb * 512 + 512])
                      for c in range(DC)] +
                     [(s_mixw[:, 8 + c:9 + c],
                       xn[:, c * TH + tb * 512: c * TH + tb * 512 + 512])
                      for c in range(DC)])
            e_g = mm(p, ps_sum[0:1, :], pairs, after=(e_na, e_gr), sig="te")
            e_gr = p.op("sc", lambda eng, tb=tb, toff=toff: eng.activation(
                grow[0:1, toff + tb * 512: toff + tb * 512 + 512], ps_sum[0:1, :],
                AF.Sigmoid, bias=s_mixb[0:1, 0:1]), after=(e_g,), sig="sc")
        if dbg and th == 0:
            d_gr = p.dout("d_gr", (1, TLOC))
            p.dma("ge", d_gr, grow[0:1, :], after=(e_gr,))
        e_st = None
        for tb in range(TH // 512):
            e_lr = None
            for c in range(DC):
                e_lr = p.dma("ge", resb[:, c * 512:(c + 1) * 512],
                             x1T[c * 128:(c + 1) * 128,
                                 toff + tb * 512: toff + tb * 512 + 512],
                             after=(e_st, e_na) if c == 0 else ())
            e_gb = mm(p, ps_bc[:],
                      [(kit.onesrow[:],
                        grow[0:1, toff + tb * 512: toff + tb * 512 + 512])],
                      after=(e_gr, e_h_free, e_st), sig="te")
            e_cmb = None
            for c in range(DC):
                ab = abelT[:, c * TH + tb * 512: c * TH + tb * 512 + 512]
                na = xn[:, c * TH + tb * 512: c * TH + tb * 512 + 512]
                rs = resb[:, c * 512:(c + 1) * 512]
                p.op("ve", lambda eng, rs=rs: eng.tensor_add(rs, rs, rs),
                     after=(e_lr,) if c == 0 else ())
                p.op("ve", lambda eng, na=na, ab=ab: eng.tensor_sub(gb[:], na, ab),
                     after=(e_na,) if c == 0 else ())
                p.op("ve", lambda eng: eng.tensor_mul(gb[:], gb[:], ps_bc[:]),
                     after=(e_gb,) if c == 0 else ())
                p.op("ve", lambda eng, rs=rs, ab=ab: eng.tensor_add(rs, rs, ab))
                e_cmb = p.op("ve", lambda eng, rs=rs: eng.tensor_add(rs, rs, gb[:]),
                             sig="ve" if c == DC - 1 else None)
            chunks = [resb[:, c * 512:(c + 1) * 512] for c in range(DC)]
            e_ln2 = emit_ln512(p, kit, chunks, chunks, ps_sum, ps_bc,
                               after=(e_cmb,))
            for c in range(DC):
                e_st = p.dma("sy", x2T[c * 128:(c + 1) * 128,
                                       toff + tb * 512: toff + tb * 512 + 512],
                             resb[:, c * 512:(c + 1) * 512],
                             after=(e_ln2,) if c == 0 else ())
        e_x2st = e_st
    p.op("ve", lambda eng: eng.engine_nop(), after=(e_x2st,))
    return p



def build_C1():
    p = Prog()
    x2T = p.din("x2T", (D, T))
    winxi = p.din("winxi", (2, DC, 128, 128))
    winz = p.din("winz", (2, DC, 128, 128))
    convw = p.din("convw", (2, 128, 4))
    convb = p.din("convb", (2, 128, 1))
    wxsl = p.din("wxsl", (2, 128, 96))
    xcT = p.dout("xcT", (CHL, T))
    szT = p.dout("szT", (CHL, T))
    xdp = p.dout("xdp", (96, T))

    s_wxi = p.sbuf("s_wxi", (128, 2 * DC * 128))
    s_wz = p.sbuf("s_wz", (128, 2 * DC * 128))
    s_cw = p.sbuf("s_cw", (128, 2 * 4))
    s_cb = p.sbuf("s_cb", (128, 2))
    s_wx = p.sbuf("s_wx", (128, 2 * 96))
    xst = [p.sbuf(f"xst{i}", (128, DC * 512)) for i in range(2)]
    xib = p.sbuf("xib", (128, 2 * 515))
    xcb = p.sbuf("xcb", (128, 2 * 512))
    szb = p.sbuf("szb", (128, 2 * 512))
    xdb = p.sbuf("xdb", (96, 512))

    ps_x = [p.psum(f"ps_x{i}", (128, 512)) for i in range(2)]
    ps_d = p.psum("ps_d", (96, 512))

    e_w = None
    for ch in range(2):
        for c in range(DC):
            p.dma("sy", s_wxi[:, (ch * DC + c) * 128:(ch * DC + c + 1) * 128],
                  winxi[ch, c])
            e_w = p.dma("sy", s_wz[:, (ch * DC + c) * 128:(ch * DC + c + 1) * 128],
                        winz[ch, c])
        p.dma("sy", s_cw[:, ch * 4:(ch + 1) * 4], convw[ch])
        p.dma("sy", s_cb[:, ch:ch + 1], convb[ch])
        e_w = p.dma("sy", s_wx[:, ch * 96:(ch + 1) * 96], wxsl[ch])
    e_halo = p.op("ve", lambda eng: eng.memset(xib[:], 0.0), sig="ve")

    NTB = T // 512
    e_stage_free = [None, None]
    e_conv_done = None
    e_z_done = None
    e_xc_st = e_sz_st = e_xd_st = None
    for tb in range(NTB):
        buf = tb % 2
        e_ld = None
        for c in range(DC):
            e_ld = p.dma("sy", xst[buf][:, c * 512:(c + 1) * 512],
                         x2T[c * 128:(c + 1) * 128, tb * 512:(tb + 1) * 512],
                         after=(e_stage_free[buf],) if c == 0 else ())
        e_xi = {}
        for ch in range(2):
            e_m = mm(p, ps_x[ch][:],
                     [(s_wxi[:, (ch * DC + c) * 128:(ch * DC + c + 1) * 128],
                       xst[buf][:, c * 512:(c + 1) * 512]) for c in range(DC)],
                     after=(e_ld, e_w, e_conv_done), sig="te")
            e_xi[ch] = p.op("sc", lambda eng, ch=ch: eng.copy(
                xib[:, ch * 515 + 3: ch * 515 + 515], ps_x[ch][:]),
                after=(e_m, e_halo), sig="sc")
        e_cv = None
        for ch in range(2):
            o = ch * 515
            e1 = p.op("ve", lambda eng, ch=ch, o=o: eng.tensor_scalar(
                xcb[:, ch * 512:(ch + 1) * 512], xib[:, o:o + 512],
                s_cw[:, ch * 4:ch * 4 + 1], None, ALU.mult), after=(e_xi[ch],))
            for j in range(1, 4):
                e1 = p.op("ve", lambda eng, ch=ch, o=o, j=j: eng.scalar_tensor_tensor(
                    xcb[:, ch * 512:(ch + 1) * 512], xib[:, o + j:o + j + 512],
                    s_cw[:, ch * 4 + j:ch * 4 + j + 1],
                    xcb[:, ch * 512:(ch + 1) * 512], ALU.mult, ALU.add))
            e_cv = p.op("ve", lambda eng, o=o: eng.tensor_copy(
                xib[:, o:o + 3], xib[:, o + 512:o + 515]),
                sig="ve" if ch == 1 else None)
        e_conv_done = e_cv
        e_si = None
        for ch in range(2):
            e_si = p.op("sc", lambda eng, ch=ch: eng.activation(
                xcb[:, ch * 512:(ch + 1) * 512], xcb[:, ch * 512:(ch + 1) * 512],
                AF.Silu, bias=s_cb[:, ch:ch + 1]), after=(e_cv, e_xc_st), sig="sc")
        for ch in range(2):
            e_xc_st = p.dma("sy", xcT[ch * 128:(ch + 1) * 128, tb * 512:(tb + 1) * 512],
                            xcb[:, ch * 512:(ch + 1) * 512], after=(e_si,))
        for ch in range(2):
            e_m = mm(p, ps_x[ch][:],
                     [(s_wz[:, (ch * DC + c) * 128:(ch * DC + c + 1) * 128],
                       xst[buf][:, c * 512:(c + 1) * 512]) for c in range(DC)],
                     after=(e_xi[ch], e_z_done), sig="te")
            e_z = p.op("sc", lambda eng, ch=ch: eng.activation(
                szb[:, ch * 512:(ch + 1) * 512], ps_x[ch][:], AF.Silu),
                after=(e_m, e_sz_st), sig="sc")
            e_sz_st = p.dma("sy", szT[ch * 128:(ch + 1) * 128, tb * 512:(tb + 1) * 512],
                            szb[:, ch * 512:(ch + 1) * 512], after=(e_z,))
            e_z_done = e_z
        e_stage_free[buf] = e_z_done
        e_dm = mm(p, ps_d[:],
                  [(s_wx[:, ch * 96:(ch + 1) * 96], xcb[:, ch * 512:(ch + 1) * 512])
                   for ch in range(2)], after=(e_si, e_xd_st), sig="te")
        e_de = p.op("sc", lambda eng: eng.copy(xdb[:], ps_d[:]), after=(e_dm,),
                    sig="sc")
        e_xd_st = p.dma("sy", xdp[:, tb * 512:(tb + 1) * 512], xdb[:],
                        after=(e_de,))
    p.op("ve", lambda eng: eng.engine_nop(), after=(e_xc_st, e_sz_st, e_xd_st))
    return p



def build_C2(dbg=0):
    p = Prog()
    xcT = p.din("xcT", (CHL, T))
    szT = p.din("szT", (CHL, T))
    xdall = p.din("xdall", (NCORE, 96, T))
    wdtsl = p.din("wdtsl", (2, DTR, 128))
    bdtsl = p.din("bdtsl", (2, 128, 1))
    alogsl = p.din("alogsl", (2, 128, 16))
    dsl = p.din("dsl", (2, 128, 1))
    woutsl = p.din("woutsl", (2, DC, 128, 128))
    selbc = p.din("selbc", (96, 2 * DST * 128))
    wop = p.dout("wop", (D, T))

    TBS = 2048
    s_wdt = p.sbuf("s_wdt", (DTR, 2 * 128))
    s_bdt = p.sbuf("s_bdt", (128, 2))
    s_dd = p.sbuf("s_dd", (128, 2))
    s_an = p.sbuf("s_an", (128, 2 * DST))
    s_wo = p.sbuf("s_wo", (128, 2 * DC * 128))
    xdsum = p.nc.dram_tensor("xdsum", [96, T], F32, kind="Internal").ap()
    xdsb = p.sbuf("xdsb", (96, TBS))
    onesr = p.sbuf("onesr", (1, 128))
    s_sel = p.sbuf("s_sel", (96, 2 * DST * 128))
    xcb = p.sbuf("xcb", (128, 2 * TBS))
    szb = p.sbuf("szb", (128, 2 * TBS))
    xstg = xcb
    xstg2 = szb
    dlt = p.sbuf("dlt", (128, 2 * TBS))
    ub = p.sbuf("ub", (128, 2 * TBS))
    bb = p.sbuf("bb", (128, TBS))
    cb = p.sbuf("cb", (128, TBS))
    dab = p.sbuf("dab", (128, TBS))
    dbu = p.sbuf("dbu", (128, TBS))
    hb = p.sbuf("hb", (128, TBS))
    yac = p.sbuf("yac", (128, 2 * TBS))
    carry = p.sbuf("carry", (128, 2 * DST))
    one1 = p.sbuf("one1C", (128, 1))
    ps_a = [p.psum(f"ps_a{i}", (128, 512)) for i in range(2)]
    ps_w = [p.psum(f"ps_w{i}", (128, 512)) for i in range(2)]

    e_c = None
    for ch in range(2):
        p.dma("sy", s_wdt[:, ch * 128:(ch + 1) * 128], wdtsl[ch])
        p.dma("sy", s_bdt[:, ch:ch + 1], bdtsl[ch])
        p.dma("sy", s_dd[:, ch:ch + 1], dsl[ch])
        p.dma("sy", s_an[:, ch * DST:(ch + 1) * DST], alogsl[ch])
        for dt in range(DC):
            e_c = p.dma("sy", s_wo[:, (ch * DC + dt) * 128:(ch * DC + dt + 1) * 128],
                        woutsl[ch, dt])
    e_ones = p.op("ve", lambda eng: eng.memset(onesr[:], 1.0), sig="ve")
    p.op("ve", lambda eng: eng.memset(one1[:], 1.0))
    e_sel = p.dma("sy", s_sel[:], selbc)
    e_zc = p.op("ve", lambda eng: eng.memset(carry[:], 0.0), sig="ve")
    e_an = p.op("sc", lambda eng: eng.activation(s_an[:], s_an[:], AF.Exp),
                after=(e_c,), sig="sc")
    e_an = p.op("ve", lambda eng: eng.tensor_scalar(
        s_an[:], s_an[:], -1.0, None, ALU.mult), after=(e_an,), sig="ve")

    if dbg == 4:
        p.op("ve", lambda eng: eng.engine_nop(), after=(e_an, e_sel, e_c))
        return p
    e_sum_st = None
    for q in range(4):
        e_a = p.dma("sy", xstg[0:96, 0:4096], xdall[0, :, q * 4096:(q + 1) * 4096],
                    after=(e_sum_st,) if e_sum_st else ())
        e_acc = None
        for r in range(1, NCORE):
            e_l = p.dma("sy", xstg2[0:96, 0:4096], xdall[r, :, q * 4096:(q + 1) * 4096],
                        after=(e_acc,) if e_acc else ())
            e_acc = p.op("ve", lambda eng: eng.tensor_add(
                xstg[0:96, 0:4096], xstg[0:96, 0:4096], xstg2[0:96, 0:4096]),
                after=(e_l, e_a), sig="ve")
        e_sum_st = p.dma("sy", xdsum[:, q * 4096:(q + 1) * 4096], xstg[0:96, 0:4096],
                         after=(e_acc,))
    e_xds_hbm = e_sum_st

    if dbg == 1:
        d_xs = p.dout("d_xds", (96, T))
        e_d = None
        for q in range(8):
            e_l = p.dma("sy", xdsb[0:96, 0:2048], xdsum[:, q * 2048:(q + 1) * 2048],
                        after=(e_xds_hbm, e_d))
            e_d = p.dma("ge", d_xs[:, q * 2048:(q + 1) * 2048], xdsb[0:96, 0:2048],
                        after=(e_l,))
        p.op("ve", lambda eng: eng.engine_nop(), after=(e_d,))
        return p
    NTB = T // TBS
    e_wo_st = None
    e_xc_free = None
    e_xds_free = None
    e_wo_ev_last = None
    for tb in range(NTB):
        t0 = tb * TBS
        e_lxc = None
        e_lsz = None
        for ch in range(2):
            e_lxc = p.dma("sy", xcb[:, ch * TBS:(ch + 1) * TBS],
                          xcT[ch * 128:(ch + 1) * 128, t0:t0 + TBS],
                          after=(e_xc_free, e_xds_hbm) if ch == 0 else ())
            e_lsz = p.dma("sy", szb[:, ch * TBS:(ch + 1) * TBS],
                          szT[ch * 128:(ch + 1) * 128, t0:t0 + TBS],
                          after=(e_xc_free,) if ch == 0 else ())
        e_xds = p.dma("sy", xdsb[:], xdsum[:, t0:t0 + TBS],
                      after=(e_xds_free,) if e_xds_free else ())
        e_dl = {}
        e_pa_free = [None, None]
        for ch in range(2):
            e_d = None
            for blk in range(TBS // 512):
                e_m = mm(p, ps_a[blk % 2][:],
                         [(s_wdt[0:DTR, ch * 128:(ch + 1) * 128],
                           xdsb[0:DTR, blk * 512: blk * 512 + 512])],
                         after=(e_xds, e_pa_free[blk % 2]), sig="te")
                e_e = p.op("sc", lambda eng, ch=ch, blk=blk: eng.activation(
                    dlt[:, ch * TBS + blk * 512: ch * TBS + blk * 512 + 512],
                    ps_a[blk % 2][:], AF.Exp, bias=s_bdt[:, ch:ch + 1]),
                    after=(e_m,), sig="sc")
                e_pa_free[blk % 2] = e_e
                e_d = p.op("sc", lambda eng, ch=ch, blk=blk: eng.activation(
                    dlt[:, ch * TBS + blk * 512: ch * TBS + blk * 512 + 512],
                    dlt[:, ch * TBS + blk * 512: ch * TBS + blk * 512 + 512],
                    AF.Ln, bias=one1[:, 0:1]), sig="sc")
            e_dl[ch] = e_d
        if dbg == 2 and tb == 0:
            d_dl = p.dout("d_dlt", (128, 2 * TBS))
            e_d = p.dma("ge", d_dl, dlt[:], after=(e_dl[0], e_dl[1]))
            p.op("ve", lambda eng: eng.engine_nop(), after=(e_d,))
            return p
        e_u = None
        for ch in range(2):
            e_u = p.op("ve", lambda eng, ch=ch: eng.tensor_mul(
                ub[:, ch * TBS:(ch + 1) * TBS], dlt[:, ch * TBS:(ch + 1) * TBS],
                xcb[:, ch * TBS:(ch + 1) * TBS]),
                after=(e_dl[ch], e_lxc), sig="ve" if ch == 1 else None)
        e_y0 = p.op("ve", lambda eng: eng.memset(yac[:], 0.0), after=(e_wo_st,),
                    sig="ve")
        e_db_prev = None
        e_hc_prev = None
        e_h_prev = e_wo_st
        e_y_prev = None
        e_pw_free = [e_wo_ev_last, e_wo_ev_last]
        for n in range(DST):
            for blk in range(TBS // 512):
                e_b = mm(p, ps_w[0][:],
                         [(s_sel[:, n * 128:(n + 1) * 128],
                           xdsb[:, blk * 512: blk * 512 + 512])],
                         after=(e_xds, e_sel, e_pw_free[0], e_db_prev), sig="te")
                e_be = p.op("sc", lambda eng, blk=blk: eng.copy(
                    bb[:, blk * 512:(blk + 1) * 512], ps_w[0][:]),
                    after=(e_b,), sig="sc")
                e_pw_free[0] = e_be
                e_cm = mm(p, ps_w[1][:],
                          [(s_sel[:, (DST + n) * 128:(DST + n + 1) * 128],
                            xdsb[:, blk * 512: blk * 512 + 512])],
                          after=(e_pw_free[1], e_hc_prev), sig="te")
                e_ce = p.op("sc", lambda eng, blk=blk: eng.copy(
                    cb[:, blk * 512:(blk + 1) * 512], ps_w[1][:]),
                    after=(e_cm,), sig="sc")
                e_pw_free[1] = e_ce
            for ch in range(2):
                e_da = p.op("sc", lambda eng, ch=ch, n=n: eng.activation(
                    dab[:], dlt[:, ch * TBS:(ch + 1) * TBS], AF.Exp,
                    scale=s_an[:, ch * DST + n: ch * DST + n + 1]),
                    after=(e_an, e_dl[ch], e_h_prev), sig="sc")
                e_db = p.op("ve", lambda eng, ch=ch: eng.tensor_mul(
                    dbu[:], ub[:, ch * TBS:(ch + 1) * TBS], bb[:]),
                    after=(e_u, e_be, e_h_prev), sig="ve")
                e_db_prev = e_db
                e_h = p.op("ve", lambda eng, ch=ch, n=n: eng.tensor_tensor_scan(
                    hb[:], dab[:], dbu[:],
                    carry[:, ch * DST + n: ch * DST + n + 1], ALU.mult, ALU.add),
                    after=(e_da, e_zc, e_y_prev), sig="ve")
                e_h_prev = e_h
                p.op("ve", lambda eng, ch=ch, n=n: eng.tensor_copy(
                    carry[:, ch * DST + n: ch * DST + n + 1], hb[:, TBS - 1:TBS]),
                    after=(e_h,))
                e_hc_prev = p.op("ve", lambda eng: eng.tensor_mul(
                    hb[:], hb[:], cb[:]), after=(e_ce,), sig="ve")
                e_y = p.op("ve", lambda eng, ch=ch: eng.tensor_add(
                    yac[:, ch * TBS:(ch + 1) * TBS],
                    yac[:, ch * TBS:(ch + 1) * TBS], hb[:]),
                    after=(e_y0,), sig="ve")
                e_y_prev = e_y
            e_bcf = [e_y, e_y]
        e_yn = e_y
        if dbg == 3 and tb == 0:
            d_y = p.dout("d_yac", (128, 2 * TBS))
            e_d = p.dma("ge", d_y, yac[:], after=(e_yn,))
            p.op("ve", lambda eng: eng.engine_nop(), after=(e_d,))
            return p
        e_ya = None
        for ch in range(2):
            p.op("ve", lambda eng, ch=ch: eng.scalar_tensor_tensor(
                yac[:, ch * TBS:(ch + 1) * TBS], xcb[:, ch * TBS:(ch + 1) * TBS],
                s_dd[:, ch:ch + 1], yac[:, ch * TBS:(ch + 1) * TBS],
                ALU.mult, ALU.add), after=(e_yn,) if ch == 0 else ())
            e_ya = p.op("ve", lambda eng, ch=ch: eng.tensor_mul(
                yac[:, ch * TBS:(ch + 1) * TBS], yac[:, ch * TBS:(ch + 1) * TBS],
                szb[:, ch * TBS:(ch + 1) * TBS]),
                after=(e_lsz,) if ch == 0 else (), sig="ve" if ch == 1 else None)
        e_xc_free = e_ya
        e_xds_free = e_yn
        e_woev = None
        for dt in range(DC):
            for blk in range(TBS // 512):
                e_m = mm(p, ps_w[blk % 2][:],
                         [(s_wo[:, (ch * DC + dt) * 128:(ch * DC + dt + 1) * 128],
                           yac[:, ch * TBS + blk * 512: ch * TBS + blk * 512 + 512])
                          for ch in range(2)],
                         after=(e_ya, e_c, e_woev) + ((e_wo_st,) if blk == 0 else ()),
                         sig="te")
                e_woev = p.op("sc", lambda eng, dt=dt, blk=blk: eng.copy(
                    dab[:, blk * 512:(blk + 1) * 512],
                    ps_w[blk % 2][:]), after=(e_m,), sig="sc")
                e_wo_ev_last = e_woev
            e_wo_st = p.dma("sy", wop[dt * 128:(dt + 1) * 128, t0:t0 + TBS],
                            dab[:, 0:TBS], after=(e_woev,))
    p.op("ve", lambda eng: eng.engine_nop(), after=(e_wo_st,))
    return p



def build_D(dbg=0):
    p = Prog()
    resT = p.din("resT", (D, TLOC))
    wps = p.din("wps", (NCORE, D, TLOC))
    router = p.din("router", (D, 8))
    egt = p.din("egt", (NEXP, 128, DC, 4, 128))
    eut = p.din("eut", (NEXP, 128, DC, 4, 128))
    edt = p.din("edt", (NEXP, DC, 128, 4, 128))
    sgt = p.din("sgt", (DC, 128, DC, 128))
    sut = p.din("sut", (DC, 128, DC, 128))
    sdt = p.din("sdt", (DC, 128, DC, 128))
    eye128 = p.din("eye128", (128, 128))
    yout = p.dout("yout", (TLOC, D))

    ident = p.sbuf("ident", (128, 128))
    s_rt = p.sbuf("s_rt", (128, DC * 8))
    x3 = p.sbuf("x3", (128, DC * TLOC))
    acc = p.sbuf("acc", (128, DC * TLOC))
    wbuf = p.sbuf("wbuf", (128, DC * 512))
    wbuf2 = p.sbuf("wbuf2", (128, DC * 512))
    wbufd = p.sbuf("wbufd", (128, 4 * 128))
    act1 = p.sbuf("act1", (128, 4 * 512))
    act2 = p.sbuf("act2", (128, 512))
    shs = p.sbuf("shs", (128, DC * 512))
    lgt = p.sbuf("lgtD", (128, 8))
    cf = p.sbuf("cf", (128, 16 * 8))
    crow = p.sbuf("crow", (1, 512))
    cbb = p.sbuf("cbb", (128, 512))
    mcol = p.sbuf("mcol", (128, 2))
    otile = p.sbuf("otile", (128, 1024))
    t2 = top2_scratch(p, "t2D")
    kit = LnKit(p, "lnD", x2s=shs)

    ps_a = [p.psum(f"ps_a{i}", (128, 512)) for i in range(2)]
    ps_b = p.psum("ps_bD", (128, 512))
    ps_sum = p.psum("ps_sumD", (1, 512))
    ps_t = p.psum("ps_tD", (128, 512))

    e_id = p.dma("sy", ident[:], eye128)
    for c in range(DC):
        p.dma("sy", s_rt[:, c * 8:(c + 1) * 8], router[c * 128:(c + 1) * 128, :])

    e_l0 = None
    for c in range(DC):
        e_l0 = p.dma("sy", x3[:, c * TLOC:(c + 1) * TLOC],
                     resT[c * 128:(c + 1) * 128, :])
    e_ac = None
    for r in range(NCORE):
        for c in range(DC):
            e_l = p.dma("sy" if r % 2 == 0 else "ge",
                        acc[:, c * TLOC:(c + 1) * TLOC],
                        wps[r, c * 128:(c + 1) * 128, :],
                        after=(e_ac,) if e_ac else ())
            e_ac = p.op("ve", lambda eng, c=c: eng.tensor_add(
                x3[:, c * TLOC:(c + 1) * TLOC], x3[:, c * TLOC:(c + 1) * TLOC],
                acc[:, c * TLOC:(c + 1) * TLOC]), after=(e_l, e_l0), sig="ve")
    e_ln = None
    for blk in range(TLOC // 512):
        chunks = [x3[:, c * TLOC + blk * 512: c * TLOC + blk * 512 + 512]
                  for c in range(DC)]
        e_ln = emit_ln512(p, kit, chunks, chunks, ps_sum, ps_b,
                          after=(e_ac, e_ln))

    if dbg == 1:
        d_x3 = p.dout("d_x3", (128, DC * TLOC))
        e_d = p.dma("ge", d_x3, x3[:], after=(e_ln,))
        p.op("ve", lambda eng: eng.engine_nop(), after=(e_d,))
        return p
    e_cfT = None
    for tt in range(TLOC // 128):
        e_gl = mm(p, ps_a[0][:, 0:8],
                  [(x3[:, c * TLOC + tt * 128: c * TLOC + (tt + 1) * 128],
                    s_rt[:, c * 8:(c + 1) * 8]) for c in range(DC)],
                  after=(e_ln, e_cfT), sig="te")
        e_lg = p.op("ve", lambda eng: eng.tensor_copy(lgt[:], ps_a[0][0:128, 0:8]),
                    after=(e_gl,), sig="ve")
        e_rm = p.op("ve", lambda eng: eng.reduce_max(
            mcol[:, 0:1], lgt[:], AX.X), sig="ve")
        e_mn = p.op("ve", lambda eng: eng.tensor_scalar(
            mcol[:, 1:2], mcol[:, 0:1], -1.0, None, ALU.mult),
            after=(e_rm,), sig="ve")
        e_ex = p.op("sc", lambda eng: eng.activation(
            lgt[:], lgt[:], AF.Exp, bias=mcol[:, 1:2],
            accum_out=mcol[:, 0:1]), after=(e_mn,), sig="sc")
        e_rc = p.op("ve", lambda eng: eng.reciprocal(mcol[:, 1:2], mcol[:, 0:1]),
                    after=(e_ex,), sig="ve")
        e_pr = p.op("ve", lambda eng: eng.tensor_scalar(
            lgt[:], lgt[:], mcol[:, 1:2], None, ALU.mult), after=(e_rc,), sig="ve")
        e_cfT = emit_top2(p, t2, lgt[:], cf[:, tt * 8:(tt + 1) * 8], "probs",
                          after=(e_pr, t2["e_ones"]))

    if dbg == 2:
        d_cf = p.dout("d_cf", (128, 16 * 8))
        e_d = p.dma("ge", d_cf, cf[:], after=(e_cfT,))
        p.op("ve", lambda eng: eng.engine_nop(), after=(e_d,))
        return p
    e_sha = None
    e_wl = None
    for tb in range(TLOC // 512):
        tsl = lambda c: x3[:, c * TLOC + tb * 512: c * TLOC + tb * 512 + 512]
        e_tile = None
        for fc in range(DC):
            e_w1 = p.dma("sy", wbuf[:, 0:1024], sgt[fc].rearrange("p c f -> p (c f)"),
                         after=(e_wl,) if e_wl else ())
            e_w2 = p.dma("sy", wbuf2[:, 0:1024], sut[fc].rearrange("p c f -> p (c f)"))
            e_mg = mm(p, ps_a[0][:],
                      [(wbuf[:, c * 128:(c + 1) * 128], tsl(c)) for c in range(DC)],
                      after=(e_w1, e_ln, e_tile, e_cfT, e_sha), sig="te")
            e_sg = p.op("sc", lambda eng: eng.activation(act2[:], ps_a[0][:], AF.Silu),
                        after=(e_mg,), sig="sc")
            e_mu = mm(p, ps_a[1][:],
                      [(wbuf2[:, c * 128:(c + 1) * 128], tsl(c)) for c in range(DC)],
                      after=(e_w2, e_sg), sig="te")
            e_tile = p.op("ve", lambda eng, fc=fc: eng.tensor_mul(
                shs[:, fc * 512:(fc + 1) * 512], ps_a[1][:], act2[:]),
                after=(e_mu,), sig="ve")
            e_wl = e_mu
        for dt in range(DC):
            e_w3 = p.dma("sy", wbuf[:, 0:1024],
                         sdt[dt].rearrange("p c f -> p (c f)"), after=(e_wl,))
            e_md = mm(p, ps_a[0][:],
                      [(wbuf[:, c * 128:(c + 1) * 128],
                        shs[:, c * 512:(c + 1) * 512]) for c in range(DC)],
                      after=(e_w3, e_tile, e_sha), sig="te")
            e_sha = p.op("ve", lambda eng, dt=dt, tb=tb: eng.tensor_add(
                acc[:, dt * TLOC + tb * 512: dt * TLOC + tb * 512 + 512],
                x3[:, dt * TLOC + tb * 512: dt * TLOC + tb * 512 + 512],
                ps_a[0][:]), after=(e_md,), sig="ve")
            e_wl = e_md

    if dbg == 3:
        d_acc = p.dout("d_acc", (128, DC * TLOC))
        e_d = p.dma("ge", d_acc, acc[:], after=(e_sha,))
        p.op("ve", lambda eng: eng.engine_nop(), after=(e_d,))
        return p
    e_free = None
    for e in range(1 if dbg in (6, 7, 9) else NEXP):
        e_lg1 = p.dma("sy", wbuf[:], egt[e].rearrange("p c f q -> p (c f q)"),
                      after=(e_free,) if e_free else (e_sha,))
        e_lu1 = p.dma("sy", wbuf2[:], eut[e].rearrange("p c f q -> p (c f q)"))
        for tb in range(TLOC // 512):
            tsl = lambda c: x3[:, c * TLOC + tb * 512: c * TLOC + tb * 512 + 512]
            e_tp = None
            for q4 in range(4):
                tt = tb * 4 + q4
                e_tp = tr(p, ps_sum[0:1, q4 * 128:(q4 + 1) * 128],
                          cf[:, tt * 8 + e: tt * 8 + e + 1], ident[:],
                          after=(e_cfT, e_free, e_id) if q4 == 0 else (),
                          sig="te" if q4 == 3 else None)
            e_cr = p.op("sc", lambda eng: eng.copy(crow[0:1, :], ps_sum[0:1, :]),
                        after=(e_tp,), sig="sc")
            e_bc0 = mm(p, ps_b[:], [(kit.onesrow[:], crow[0:1, :])],
                       after=(e_cr,), sig="te")
            e_bc = p.op("sc", lambda eng: eng.copy(cbb[:], ps_b[:]),
                        after=(e_bc0,), sig="sc")
            e_a1 = None
            for ft in range(4):
                e_mg = mm(p, ps_a[ft % 2][:],
                          [(wbuf[:, (c * 4 + ft) * 128:(c * 4 + ft + 1) * 128],
                            tsl(c)) for c in range(DC)],
                          after=(e_lg1, e_ln, e_a1, e_sha), sig="te")
                e_sg = p.op("sc", lambda eng, ft=ft: eng.activation(
                    act1[:, ft * 512:(ft + 1) * 512], ps_a[ft % 2][:], AF.Silu),
                    after=(e_mg,), sig="sc")
                e_mu = mm(p, ps_a[ft % 2][:],
                          [(wbuf2[:, (c * 4 + ft) * 128:(c * 4 + ft + 1) * 128],
                            tsl(c)) for c in range(DC)],
                          after=(e_lu1, e_sg), sig="te")
                e_a1 = p.op("ve", lambda eng, ft=ft: eng.tensor_mul(
                    act1[:, ft * 512:(ft + 1) * 512], ps_a[ft % 2][:],
                    act1[:, ft * 512:(ft + 1) * 512]), after=(e_mu,), sig="ve")
            e_dacc = e_free
            for dt in range(DC):
                e_ld = p.dma("ge", wbufd[:], edt[e, dt].rearrange("p f q -> p (f q)"),
                             after=(e_dacc,))
                e_md = mm(p, ps_a[0][:],
                          [(wbufd[:, ft * 128:(ft + 1) * 128],
                            act1[:, ft * 512:(ft + 1) * 512]) for ft in range(4)],
                          after=(e_ld, e_a1, e_dacc), sig="te")
                e_mlt = p.op("ve", lambda eng: eng.tensor_mul(
                    act2[:], ps_a[0][:], cbb[:]), after=(e_md, e_bc),
                    sig="ve" if dbg == 9 else None)
                if dbg == 9 and e == 0 and tb == 3 and dt == 7:
                    d_act2 = p.dout("d_act2", (128, 512))
                    p.dma("sy", d_act2, act2[:], after=(e_mlt,))
                e_dacc = p.op("ve", lambda eng, dt=dt, tb=tb: eng.tensor_add(
                    acc[:, dt * TLOC + tb * 512: dt * TLOC + tb * 512 + 512],
                    acc[:, dt * TLOC + tb * 512: dt * TLOC + tb * 512 + 512],
                    act2[:]), after=(e_sha,), sig="ve")
            e_free = e_dacc

    if dbg in (4, 6, 9):
        d_acc = p.dout("d_acc", (128, DC * TLOC))
        e_f1 = p.op("ve", lambda eng: eng.memset(act1[:], 0.0), after=(e_free,),
                    sig="ve")
        e_f1 = p.op("ve", lambda eng: eng.memset(act1[:], 1.0), sig="ve")
        e_f1 = p.op("ve", lambda eng: eng.memset(act1[:], 2.0), sig="ve")
        e_d = p.dma("sy", d_acc, acc[:], after=(e_f1,))
        p.op("ve", lambda eng: eng.engine_nop(), after=(e_d,))
        return p
    e_st = None
    e_lnf = None
    for blk in range(TLOC // 512):
        chunks = [acc[:, c * TLOC + blk * 512: c * TLOC + blk * 512 + 512]
                  for c in range(DC)]
        e_lnf = emit_ln512(p, kit, chunks, chunks, ps_sum, ps_b,
                           after=(e_free, e_lnf))
        for q in range(4):
            tt = blk * 4 + q
            e_tp = None
            e_ev = None
            for c in range(DC):
                e_tp = tr(p, ps_t[:, (c % 4) * 128:(c % 4) * 128 + 128],
                          acc[:, c * TLOC + tt * 128: c * TLOC + (tt + 1) * 128],
                          ident[:],
                          after=(e_lnf, e_st) if c == 0 else
                                ((e_ev,) if c == 4 else ()),
                          sig="te" if c % 4 == 3 else None)
                if c % 4 == 3:
                    e_ev = p.op("sc", lambda eng, c=c: eng.copy(
                        otile[:, (c - 3) * 128:(c + 1) * 128], ps_t[:, :]),
                        after=(e_tp,), sig="sc")
            e_st = p.dma("sy", yout[tt * 128:(tt + 1) * 128, :], otile[:],
                         after=(e_ev,))
    p.op("ve", lambda eng: eng.engine_nop(), after=(e_st,))
    return p



def _f(a):
    return np.ascontiguousarray(a, dtype=np.float32)


def kernel(**inp):
    x = _f(inp["L0_attn_x"])[0]
    xT = _f(x.T)

    eye128 = np.eye(128, dtype=np.float32)
    eyeD = np.eye(D, dtype=np.float32)
    iotaf = np.broadcast_to(np.arange(512, dtype=np.float32), (128, 512)).copy()

    pa = build_A()
    nca = pa.build()
    in_maps = []
    for c in range(NCORE):
        qgc = (2048 * c + 128 * np.arange(16)[None, :] +
               np.arange(128)[:, None]).astype(np.float32)
        qgkbc = (qgc[:, :, None] - 512.0 * np.arange(32, dtype=np.float32)[None, None, :]
                 ).reshape(128, 512).astype(np.float32)
        in_maps.append(dict(
            xT=xT, xownT=_f(xT[:, c * TLOC:(c + 1) * TLOC]), qgkb=qgkbc,
            wqT=_f(inp["L0_attn_Wq"].T), wkT=_f(inp["L0_attn_Wk"].T),
            wvT=_f(inp["L0_attn_Wov"].T), woT=_f(inp["L0_attn_Wo"].T),
            bq=_f(inp["L0_attn_bq"]).reshape(DH, 1),
            bk=_f(inp["L0_attn_bk"]).reshape(DH, 1),
            bv=_f(inp["L0_attn_bov"]).reshape(DH, 1),
            bo=_f(inp["L0_attn_bo"]).reshape(DC, 128, 1),
            qg=qgc, iotaf=iotaf, eye128=eye128))
    res = run_bass_kernel_spmd(nca, in_maps, core_ids=list(range(NCORE)))
    x1T = np.concatenate([r["x1T"] for r in res.results], axis=1)

    w1 = _f(inp["L0_ffn_w1"])
    w2 = _f(inp["L0_ffn_w2"])
    pb = build_B()
    ncb = pb.build()
    com = dict(
        lng=_f(inp["L0_ffn_ln_g"]).reshape(DC, 128, 1),
        lnb=_f(inp["L0_ffn_ln_b"]).reshape(DC, 128, 1),
        abelW=_f(inp["L0_ffn_abel_w"]).reshape(DC, 128, DC, 128)
            .transpose(2, 1, 0, 3).copy(),
        abelB=_f(inp["L0_ffn_abel_b"]).reshape(DC, 128, 1),
        gateW=_f(inp["L0_ffn_gate_w_moe"]),
        gateBB=np.broadcast_to(_f(inp["L0_ffn_gate_b_moe"]), (128, 8)).copy(),
        w1t=w1.reshape(NEXP, DC, 128, 8, 512).transpose(0, 3, 2, 1, 4).copy(),
        b1r=_f(inp["L0_ffn_b1"]).reshape(NEXP, 32, 128).transpose(0, 2, 1).copy(),
        w2t=w2.reshape(NEXP, 8, 4, 128, DC, 128).transpose(0, 1, 3, 2, 4, 5).copy(),
        b2t=_f(inp["L0_ffn_b2"]).reshape(NEXP, DC, 128).transpose(2, 0, 1).reshape(128, NEXP * DC).copy(),
        gens=_f(inp["L0_ffn_gens"]),
        gensT=_f(inp["L0_ffn_gens"].transpose(0, 2, 1)), eye=eyeD, eye128=eye128,
        mixw=_f(inp["L0_ffn_mix_w"]), mixb=_f(inp["L0_ffn_mix_b"]).reshape(1, 1))
    in_maps = [dict(com, x1T=_f(x1T[:, c * TLOC:(c + 1) * TLOC]))
               for c in range(NCORE)]
    res = run_bass_kernel_spmd(ncb, in_maps, core_ids=list(range(NCORE)))
    x2T = np.concatenate([r["x2T"] for r in res.results], axis=1)

    win = _f(inp["L1_attn_W_in"])
    pc1 = build_C1()
    ncc1 = pc1.build()
    in_maps = []
    for c in range(NCORE):
        ch0 = c * CHL
        wxi = win[:, ch0:ch0 + CHL]
        wz = win[:, DIN + ch0:DIN + ch0 + CHL]
        in_maps.append(dict(
            x2T=x2T,
            winxi=wxi.reshape(DC, 128, 2, 128).transpose(2, 0, 1, 3).copy(),
            winz=wz.reshape(DC, 128, 2, 128).transpose(2, 0, 1, 3).copy(),
            convw=_f(inp["L1_attn_conv_w"])[ch0:ch0 + CHL].reshape(2, 128, 4),
            convb=_f(inp["L1_attn_conv_b"])[ch0:ch0 + CHL].reshape(2, 128, 1),
            wxsl=_f(inp["L1_attn_W_x"])[ch0:ch0 + CHL].reshape(2, 128, 96)))
    res = run_bass_kernel_spmd(ncc1, in_maps, core_ids=list(range(NCORE)))
    xc_all = [r["xcT"] for r in res.results]
    sz_all = [r["szT"] for r in res.results]
    xd_all = np.stack([r["xdp"] for r in res.results])

    pc2 = build_C2()
    ncc2 = pc2.build()
    selbc = np.zeros((96, 2 * DST, 128), np.float32)
    for j in range(2 * DST):
        selbc[64 + j, j, :] = 1.0
    selbc = selbc.reshape(96, 2 * DST * 128)
    in_maps = []
    wdt = _f(inp["L1_attn_W_dt"])
    wout = _f(inp["L1_attn_W_out"])
    for c in range(NCORE):
        ch0 = c * CHL
        in_maps.append(dict(
            xcT=xc_all[c], szT=sz_all[c], xdall=xd_all, selbc=selbc,
            wdtsl=wdt[:, ch0:ch0 + CHL].reshape(DTR, 2, 128).transpose(1, 0, 2).copy(),
            bdtsl=_f(inp["L1_attn_b_dt"])[ch0:ch0 + CHL].reshape(2, 128, 1),
            alogsl=_f(inp["L1_attn_A_log"])[ch0:ch0 + CHL].reshape(2, 128, DST),
            dsl=_f(inp["L1_attn_D"])[ch0:ch0 + CHL].reshape(2, 128, 1),
            woutsl=wout[ch0:ch0 + CHL].reshape(2, 128, DC, 128)
                .transpose(0, 2, 1, 3).copy()))
    res = run_bass_kernel_spmd(ncc2, in_maps, core_ids=list(range(NCORE)))
    wop_all = [r["wop"] for r in res.results]

    pd = build_D()
    ncd = pd.build()
    eg = _f(inp["L1_ffn_experts_gate"])
    eu = _f(inp["L1_ffn_experts_up"])
    ed = _f(inp["L1_ffn_experts_down"])
    sg = _f(inp["L1_ffn_shared_gate"])
    su = _f(inp["L1_ffn_shared_up"])
    sd = _f(inp["L1_ffn_shared_down"])
    com = dict(
        router=_f(inp["L1_ffn_router"]),
        egt=eg.reshape(NEXP, DC, 128, 4, 128).transpose(0, 2, 1, 3, 4).copy(),
        eut=eu.reshape(NEXP, DC, 128, 4, 128).transpose(0, 2, 1, 3, 4).copy(),
        edt=ed.reshape(NEXP, 4, 128, DC, 128).transpose(0, 3, 2, 1, 4).copy(),
        sgt=sg.reshape(DC, 128, DC, 128).transpose(2, 1, 0, 3).copy(),
        sut=su.reshape(DC, 128, DC, 128).transpose(2, 1, 0, 3).copy(),
        sdt=sd.reshape(DC, 128, DC, 128).transpose(2, 1, 0, 3).copy(),
        eye128=eye128)
    in_maps = []
    for c in range(NCORE):
        wps = np.stack([w[:, c * TLOC:(c + 1) * TLOC] for w in wop_all])
        in_maps.append(dict(com, resT=_f(x2T[:, c * TLOC:(c + 1) * TLOC]),
                            wps=_f(wps)))
    res = run_bass_kernel_spmd(ncd, in_maps, core_ids=list(range(NCORE)))
    y = np.concatenate([r["yout"] for r in res.results], axis=0)
    return y.reshape(1, T, D)


# version 03 (final)
# speedup vs baseline: 1.1136x; 1.1136x over previous
import math
from contextlib import ExitStack

import numpy as np

import concourse.bass as bass
import concourse.mybir as mybir
from concourse.bass_utils import run_bass_kernel_spmd

F32 = mybir.dt.float32
AF = mybir.ActivationFunctionType
ALU = mybir.AluOpType
AX = mybir.AxisListType

NCORE = 8
T = 16384
TLOC = T // NCORE
D = 1024
DC = D // 128
DH = 64
DF = 4096
NEXP = 8
DIN = 2048
DST = 16
DTR = 64
CHL = DIN // NCORE
NEG = -1.0e30
EPS = 1e-5
SCL = 1.0 / 32.0
EXPM_S = 6
EXPM_DEG = 8



class Prog:

    def __init__(self):
        self.nc = bass.Bass("TRN2", target_bir_lowering=False)
        self.es = ExitStack()
        self.ops = {c: [] for c in ("sy", "ge", "ve", "sc", "te")}
        self.sems = {}
        self.scount = {}

    def sem(self, key):
        if key not in self.sems:
            self.sems[key] = self.es.enter_context(self.nc.semaphore(key))
            self.scount[key] = 0
        return self.sems[key]

    def sbuf(self, name, shape, dtype=F32):
        return self.es.enter_context(self.nc.sbuf_tensor(name, list(shape), dtype))

    def psum(self, name, shape, dtype=F32):
        return self.es.enter_context(self.nc.psum_tensor(name, list(shape), dtype))

    def din(self, name, shape, dtype=F32):
        return self.nc.dram_tensor(name, list(shape), dtype, kind="ExternalInput").ap()

    def dout(self, name, shape, dtype=F32):
        return self.nc.dram_tensor(name, list(shape), dtype, kind="ExternalOutput").ap()

    def op(self, ch, fn, after=(), sig=None, amt=1):
        ev = None
        if sig is not None:
            self.sem(sig)
            self.scount[sig] += amt
            ev = (sig, self.scount[sig])
        waits = []
        for e in after:
            if e is None:
                continue
            self.sem(e[0])
            waits.append(e)
        self.ops[ch].append((fn, waits, (sig, amt) if sig else None))
        return ev

    def dma(self, ch, out, in_, after=()):
        return self.op(ch, lambda eng, o=out, i=in_: eng.dma_start(out=o, in_=i),
                       after=after, sig="d" + ch, amt=16)

    def build(self):
        nc = self.nc
        sems = self.sems
        ops = self.ops
        with nc.Block() as block:
            def mk(ch):
                def run(eng):
                    waited = {}
                    for fn, waits, sig in ops[ch]:
                        for (k, v) in waits:
                            if waited.get(k, 0) >= v:
                                continue
                            eng.wait_ge(sems[k], v)
                            waited[k] = v
                        inst = fn(eng)
                        if sig is not None:
                            k, amt = sig
                            inst.then_inc(sems[k], amt)
                return run
            block.sync(mk("sy"))
            block.gpsimd(mk("ge"))
            block.vector(mk("ve"))
            block.scalar(mk("sc"))
            block.tensor(mk("te"))
        self.es.close()
        return nc


def mm(p, psum_ap, pairs, after=(), sig=None):
    ev = None
    n = len(pairs)
    for i, (l, r) in enumerate(pairs):
        ev = p.op(
            "te",
            lambda eng, l=l, r=r, st=(i == 0), sp=(i == n - 1): eng.matmul(
                psum_ap, l, r, start=st, stop=sp),
            after=after if i == 0 else (),
            sig=sig if i == n - 1 else None)
    return ev


def tr(p, psum_ap, in_ap, ident_ap, after=(), sig=None):
    return p.op("te", lambda eng, o=psum_ap, i=in_ap, d=ident_ap:
                eng.transpose(o, i, d), after=after, sig=sig)


class LnKit:

    def __init__(self, p, tag, x2s=None):
        self.x2s = x2s if x2s is not None else p.sbuf(f"{tag}x2", (128, DC * 512))
        self.r0 = p.sbuf(f"{tag}r0", (1, 512))
        self.r1 = p.sbuf(f"{tag}r1", (1, 512))
        self.r2 = p.sbuf(f"{tag}r2", (1, 512))
        self.onescol = p.sbuf(f"{tag}oc", (128, 1))
        self.onesrow = p.sbuf(f"{tag}or", (1, 128))
        self.epsc = p.sbuf(f"{tag}ep", (1, 1))
        self.one1 = p.sbuf(f"{tag}o1", (128, 1))
        self.e_oc = p.op("ve", lambda eng: eng.memset(self.onescol[:], 1.0), sig="ve")
        self.e_or = p.op("ve", lambda eng: eng.memset(self.onesrow[:], 1.0), sig="ve")
        p.op("ve", lambda eng: eng.memset(self.epsc[:], EPS))
        p.op("ve", lambda eng: eng.memset(self.one1[:], 1.0))


def emit_ln512(p, kit, ins, outs, ps_sum, ps_b, after=(), gamma=None, beta=None):
    e_sq = None
    for c in range(DC):
        e_sq = p.op("sc", lambda eng, c=c: eng.activation(
            kit.x2s[:, c * 512:(c + 1) * 512], ins[c], AF.Square),
            after=after if c == 0 else (), sig="sc" if c == DC - 1 else None)
    e_m1 = mm(p, ps_sum[0:1, :], [(kit.onescol[:], ins[c]) for c in range(DC)],
              after=tuple(after) + (kit.e_oc,), sig="te")
    e_mu = p.op("sc", lambda eng: eng.mul(kit.r0[:], ps_sum[0:1, :], 1.0 / D),
                after=(e_m1,), sig="sc")
    e_m2 = mm(p, ps_sum[0:1, :],
              [(kit.onescol[:], kit.x2s[:, c * 512:(c + 1) * 512]) for c in range(DC)],
              after=(e_sq, e_mu), sig="te")
    e_s2 = p.op("sc", lambda eng: eng.mul(kit.r1[:], ps_sum[0:1, :], 1.0 / D),
                after=(e_m2,), sig="sc")
    e_v0 = p.op("ve", lambda eng: eng.tensor_mul(kit.r2[:], kit.r0[:], kit.r0[:]),
                after=(e_mu,), sig="ve")
    e_v1 = p.op("ve", lambda eng: eng.tensor_sub(kit.r1[:], kit.r1[:], kit.r2[:]),
                after=(e_s2, e_v0), sig="ve")
    e_l = p.op("sc", lambda eng: eng.activation(kit.r2[:], kit.r1[:], AF.Ln,
                                                bias=kit.epsc[0:1, 0:1]),
               after=(e_v1,))
    e_r = p.op("sc", lambda eng: eng.activation(kit.r2[:], kit.r2[:], AF.Exp,
                                                scale=-0.5), sig="sc")
    e_b1 = mm(p, ps_b[:, :], [(kit.onesrow[:], kit.r0[:])],
              after=(e_mu, kit.e_or), sig="te")
    e_sub = None
    for c in range(DC):
        e_sub = p.op("ve", lambda eng, c=c: eng.tensor_sub(outs[c], ins[c], ps_b[:, :]),
                     after=(e_b1,) if c == 0 else (),
                     sig="ve" if c == DC - 1 else None)
    e_b2 = mm(p, ps_b[:, :], [(kit.onesrow[:], kit.r2[:])],
              after=(e_r, e_sub), sig="te")
    e_fin = None
    for c in range(DC):
        e_fin = p.op("ve", lambda eng, c=c: eng.tensor_mul(outs[c], outs[c], ps_b[:, :]),
                     after=(e_b2,) if c == 0 else (),
                     sig="ve" if (gamma is None and c == DC - 1) else None)
    if gamma is not None:
        for c in range(DC):
            e_fin = p.op("ve", lambda eng, c=c: eng.tensor_scalar(
                outs[c], outs[c], gamma[c], beta[c], ALU.mult, ALU.add),
                sig="ve" if c == DC - 1 else None)
    return e_fin


def emit_top2(p, s, tile, wout, mode, after=()):
    prev = [None]

    def v(fn, extra=()):
        prev[0] = p.op("ve", fn, after=(prev[0],) + tuple(extra), sig="ve")
        return prev[0]

    v(lambda eng: eng.reduce_max(s["m1"], tile, AX.X), extra=after)
    v(lambda eng: eng.tensor_scalar(s["eq"], tile, s["m1"], None, ALU.is_equal))
    v(lambda eng: eng.tensor_tensor_scan(s["cs"], s["ones8"], s["eq"], 0.0,
                                         ALU.mult, ALU.add))
    v(lambda eng: eng.tensor_scalar(s["cs"], s["cs"], 1.0, None, ALU.is_equal))
    v(lambda eng: eng.tensor_mul(s["sel1"], s["eq"], s["cs"]))
    v(lambda eng: eng.scalar_tensor_tensor(s["l2"], s["sel1"], NEG, tile,
                                           ALU.mult, ALU.add))
    v(lambda eng: eng.reduce_max(s["m2"], s["l2"], AX.X))
    v(lambda eng: eng.tensor_scalar(s["eq"], s["l2"], s["m2"], None, ALU.is_equal))
    v(lambda eng: eng.tensor_tensor_scan(s["cs"], s["ones8"], s["eq"], 0.0,
                                         ALU.mult, ALU.add))
    v(lambda eng: eng.tensor_scalar(s["cs"], s["cs"], 1.0, None, ALU.is_equal))
    e_sel2 = v(lambda eng: eng.tensor_mul(s["sel2"], s["eq"], s["cs"]))
    if mode == "softmax":
        v(lambda eng: eng.tensor_sub(s["t1"], s["m2"], s["m1"]))
        e_e2 = p.op("sc", lambda eng: eng.activation(s["t1"], s["t1"], AF.Exp),
                    after=(prev[0],), sig="sc")
        e_z = p.op("sc", lambda eng: eng.activation(
            s["m2"], s["t1"], AF.Identity, bias=s["one1"]), sig="sc")
        v(lambda eng: eng.reciprocal(s["m1"], s["m2"]), extra=(e_z,))
        v(lambda eng: eng.tensor_mul(s["m2"], s["t1"], s["m1"]))
        v(lambda eng: eng.tensor_scalar(wout, s["sel1"], s["m1"], None, ALU.mult))
        return v(lambda eng: eng.scalar_tensor_tensor(wout, s["sel2"], s["m2"], wout,
                                                      ALU.mult, ALU.add))
    else:
        v(lambda eng: eng.tensor_scalar(wout, s["sel1"], s["m1"], None, ALU.mult))
        return v(lambda eng: eng.scalar_tensor_tensor(wout, s["sel2"], s["m2"], wout,
                                                      ALU.mult, ALU.add))


def top2_scratch(p, tag):
    mk8 = lambda n: p.sbuf(f"{tag}{n}", (128, 8))[:]
    mk1 = lambda n: p.sbuf(f"{tag}{n}", (128, 1))[:]
    s = dict(ones8=mk8("o8"), eq=mk8("eq"), cs=mk8("cs"), sel1=mk8("s1"),
             sel2=mk8("s2"), l2=mk8("l2"), m1=mk1("m1"), m2=mk1("m2"), t1=mk1("t1"),
             one1=mk1("o1"))
    s["e_ones"] = p.op("ve", lambda eng: eng.memset(s["ones8"], 1.0), sig="ve")
    p.op("ve", lambda eng: eng.memset(s["one1"], 1.0))
    return s



def build_A(dbg=False, dbg2=False):
    p = Prog()
    xT = p.din("xT", (D, T))
    xownH = p.din("xownT", (D, TLOC))
    wqT = p.din("wqT", (D, DH))
    wkT = p.din("wkT", (D, DH))
    wvT = p.din("wvT", (D, DH))
    woT = p.din("woT", (DH, D))
    bq = p.din("bq", (DH, 1))
    bk = p.din("bk", (DH, 1))
    bv = p.din("bv", (DH, 1))
    bo = p.din("bo", (DC, 128, 1))
    qg = p.din("qg", (128, 16))
    qgkb = p.din("qgkb", (128, 16 * 32))
    iotaf = p.din("iotaf", (128, 512))
    eye = p.din("eye128", (128, 128))
    x1T = p.dout("x1T", (D, TLOC))

    ident = p.sbuf("ident", (128, 128))
    s_wq = p.sbuf("s_wq", (128, DC * DH))
    s_wk = p.sbuf("s_wk", (128, DC * DH))
    s_wv = p.sbuf("s_wv", (128, DC * DH))
    s_wo = p.sbuf("s_wo", (DH, D))
    s_bq = p.sbuf("s_bq", (DH, 1))
    s_bk = p.sbuf("s_bk", (DH, 1))
    s_bv = p.sbuf("s_bv", (DH, 1))
    s_bo = p.sbuf("s_bo", (128, DC))
    s_qg = p.sbuf("s_qg", (128, 16))
    s_qgkb = p.sbuf("s_qgkb", (128, 16 * 32))
    s_iota = p.sbuf("s_iota", (128, 512))
    xst0 = p.sbuf("xst0", (128, DC * 512))
    xst = [xst0, xst0]
    kT = p.sbuf("kT", (DH, T))
    qTt = p.sbuf("qTt", (DH, TLOC))
    vst = p.sbuf("vst", (DH, 512))
    vtok = p.sbuf("vtok", (128, (T // 128) * DH))
    erow = p.sbuf("erow", (128, T // 4))
    etile = [p.sbuf(f"etile{i}", (128, 128)) for i in range(2)]
    m01 = p.sbuf("m01", (128, 512))
    t128 = p.sbuf("t128", (128, 6))
    rrow = p.sbuf("rrow", (1, 128))
    avq = p.sbuf("avq", (DH, 128))
    avall = p.sbuf("avall", (DH, TLOC))
    attnb = p.sbuf("attnb", (128, 512))
    xres = p.sbuf("xres", (128, DC * 512))
    kit = LnKit(p, "lnA", x2s=xst[0])

    ps_kv = p.psum("ps_kv", (DH, 512))
    ps_sc = [p.psum(f"ps_sc{i}", (128, 512)) for i in range(2)]
    ps_tr = p.psum("ps_tr", (128, 256))
    ps_av = p.psum("ps_av", (DH, 128))
    ps_b = p.psum("ps_b", (128, 512))
    ps_sum = p.psum("ps_sum", (1, 512))

    e_id = p.dma("sy", ident[:], eye)
    for (dst, src) in ((s_wq, wqT), (s_wk, wkT), (s_wv, wvT)):
        for c in range(DC):
            p.dma("sy", dst[:, c * DH:(c + 1) * DH], src[c * 128:(c + 1) * 128, :])
    p.dma("sy", s_wo[:], woT)
    p.dma("sy", s_bq[:], bq)
    p.dma("sy", s_bk[:], bk)
    p.dma("sy", s_bv[:], bv)
    for c in range(DC):
        p.dma("sy", s_bo[:, c:c + 1], bo[c])
    p.dma("sy", s_qg[:], qg)
    p.dma("sy", s_qgkb[:], qgkb)
    e_wload = p.dma("sy", s_iota[:], iotaf)

    NKB = T // 512
    e_stage_free = None
    e_kv_bank_free = None
    e_vst_free = None
    e_vtok = None
    e_kevac = None
    for kb in range(NKB):
        buf = 0
        e_ld = None
        for c in range(DC):
            e_ld = p.dma("sy", xst[buf][:, c * 512:(c + 1) * 512],
                         xT[c * 128:(c + 1) * 128, kb * 512:(kb + 1) * 512],
                         after=(e_stage_free,) if c == 0 else ())
        pairs_k = [(s_wk[:, c * DH:(c + 1) * DH], xst[buf][:, c * 512:(c + 1) * 512])
                   for c in range(DC)]
        e_mmk = mm(p, ps_kv[:], pairs_k, after=(e_ld, e_wload, e_kv_bank_free),
                   sig="te")
        e_kevac = p.op("sc", lambda eng, kb=kb: eng.activation(
            kT[:, kb * 512:(kb + 1) * 512], ps_kv[:], AF.Identity,
            bias=s_bk[:, 0:1]), after=(e_mmk,), sig="sc")
        pairs_v = [(s_wv[:, c * DH:(c + 1) * DH], xst[buf][:, c * 512:(c + 1) * 512])
                   for c in range(DC)]
        e_mmv = mm(p, ps_kv[:], pairs_v, after=(e_kevac, e_vst_free), sig="te")
        e_vevac = p.op("sc", lambda eng: eng.activation(
            vst[:], ps_kv[:], AF.Identity, bias=s_bv[:, 0:1]),
            after=(e_mmv,), sig="sc")
        e_trv = None
        for j in range(4):
            e_trv = tr(p, ps_tr[:, j * DH:(j + 1) * DH],
                       vst[0:DH, j * 128:(j + 1) * 128], ident[0:DH, 0:DH],
                       after=(e_vevac, e_id, e_vtok) if j == 0 else (),
                       sig="te" if j == 3 else None)
        e_vtok = p.op("sc", lambda eng, kb=kb: eng.copy(
            vtok[:, kb * 4 * DH:(kb + 1) * 4 * DH], ps_tr[:, :]),
            after=(e_trv,), sig="sc")
        e_vst_free = e_trv
        e_kv_bank_free = e_vevac
        e_stage_free = e_mmv
    e_vtok_all = e_vtok

    e_q = None
    e_qstage_free = None
    for tb in range(TLOC // 512):
        e_ld = None
        for c in range(DC):
            e_ld = p.dma("sy", xres[:, c * 512:(c + 1) * 512],
                         xownH[c * 128:(c + 1) * 128, tb * 512:(tb + 1) * 512],
                         after=(e_qstage_free,) if c == 0 else ())
        pairs_q = [(s_wq[:, c * DH:(c + 1) * DH], xres[:, c * 512:(c + 1) * 512])
                   for c in range(DC)]
        e_mmq = mm(p, ps_kv[:], pairs_q, after=(e_ld, e_kv_bank_free), sig="te")
        e_kv_bank_free = None
        e_q = p.op("sc", lambda eng, tb=tb: eng.activation(
            qTt[:, tb * 512:(tb + 1) * 512], ps_kv[:], AF.Identity,
            bias=s_bq[:, 0:1]), after=(e_mmq,), sig="sc")
        e_qstage_free = e_mmq
    e_q_all = e_q

    NHB = (T // 4) // 512
    e_avq = None
    e_evac_prev = [None, None]
    e_avmm_prev = None
    e_avused_prev = None
    for qt in range(16):
        qcol = s_qg[:, qt:qt + 1]
        e_ini = p.op("ve", lambda eng: eng.memset(t128[:, 2:3], -3.0e38))
        e_ini = p.op("ve", lambda eng: eng.memset(t128[:, 3:4], 0.0), sig="ve")
        e_avz = p.op("ve", lambda eng: eng.memset(avq[:], 0.0),
                     after=(e_avq,), sig="ve")
        for half in range(4):
            e_evac = None
            for kb in range(NHB):
                b = kb % 2
                kbg = half * NHB + kb
                e_mms = mm(p, ps_sc[b][:],
                           [(qTt[0:DH, qt * 128:(qt + 1) * 128],
                             kT[0:DH, kbg * 512:(kbg + 1) * 512])],
                           after=(e_q_all, e_kevac, e_evac_prev[b]), sig="te")
                p.op("ve", lambda eng, qt=qt, kbg=kbg: eng.tensor_scalar(
                    m01[:], s_iota[:], s_qgkb[:, qt * 32 + kbg: qt * 32 + kbg + 1],
                    None, ALU.is_gt), after=(e_wload,))
                e_evac = p.op("ve", lambda eng, kb=kb, b=b: eng.scalar_tensor_tensor(
                    erow[:, kb * 512:(kb + 1) * 512], m01[:], NEG, ps_sc[b][:],
                    ALU.mult, ALU.add),
                    after=(e_mms, e_avmm_prev) if kb == 0 else (e_mms,), sig="ve")
                e_evac_prev[b] = e_evac
                if dbg2 == 2 and qt == 0 and half == 0 and kb == 1:
                    d_m01 = p.dout("d_m01", (128, 512))
                    d_t0 = p.dout("d_t0", (128, 1))
                    d_eseg = p.dout("d_eseg", (128, 1024))
                    p.dma("ge", d_m01, m01[:], after=(e_evac,))
                    p.dma("ge", d_t0, t128[:, 0:1], after=(e_evac,))
                    e_dbg3 = p.dma("ge", d_eseg, erow[:, 0:1024], after=(e_evac,))
                    p.op("ve", lambda eng: eng.engine_nop(), after=(e_dbg3,))
                    return p

            e_c0 = p.op("ve", lambda eng: eng.reduce_max(
                t128[:, 0:1], erow[:], AX.X), sig="ve")
            e_c1 = p.op("ve", lambda eng: eng.tensor_sub(
                t128[:, 4:5], t128[:, 2:3], t128[:, 0:1]), after=(e_c0,), sig="ve")
            e_c2 = p.op("ve", lambda eng: eng.tensor_scalar_min(
                t128[:, 4:5], t128[:, 4:5], 0.0), after=(e_c1,), sig="ve")
            e_c3 = p.op("ve", lambda eng: eng.tensor_max(
                t128[:, 2:3], t128[:, 2:3], t128[:, 0:1]), after=(e_c2,), sig="ve")
            e_bias = p.op("ve", lambda eng: eng.tensor_scalar(
                t128[:, 1:2], t128[:, 2:3], -SCL, None, ALU.mult),
                after=(e_c3,), sig="ve")
            e_alpha = p.op("sc", lambda eng: eng.activation(
                t128[:, 4:5], t128[:, 4:5], AF.Exp, scale=SCL),
                after=(e_bias,), sig="sc")
            e_exp = p.op("sc", lambda eng: eng.activation(
                erow[:], erow[:], AF.Exp, bias=t128[:, 1:2], scale=SCL,
                accum_out=t128[:, 5:6]), sig="sc")
            e_l = p.op("ve", lambda eng: eng.scalar_tensor_tensor(
                t128[:, 3:4], t128[:, 3:4], t128[:, 4:5], t128[:, 5:6],
                ALU.mult, ALU.add), after=(e_exp,), sig="ve")
            e_tra = tr(p, ps_sum[0:1, 0:128], t128[:, 4:5], ident[:],
                       after=(e_alpha, e_avused_prev), sig="te")
            e_rr = p.op("sc", lambda eng: eng.copy(rrow[:], ps_sum[0:1, 0:128]),
                        after=(e_tra,), sig="sc")
            e_ab = mm(p, ps_b[0:DH, 0:128], [(kit.onesrow[0:1, 0:DH], rrow[:])],
                      after=(e_rr, kit.e_or), sig="te")
            e_avs = p.op("ve", lambda eng: eng.tensor_mul(
                avq[:], avq[:], ps_b[0:DH, 0:128]), after=(e_ab, e_avz), sig="ve")
            e_slot = [None, None]
            e_avmm = None
            for kt in range(NHB * 4):
                b = kt % 2
                ktg = half * NHB * 4 + kt
                e_trE = tr(p, ps_tr[:, b * 128:(b + 1) * 128],
                           erow[:, kt * 128:(kt + 1) * 128], ident[:],
                           after=(e_exp, e_slot[b]), sig="te")
                e_ev = p.op("sc", lambda eng, b=b: eng.copy(
                    etile[b][:], ps_tr[:, b * 128:(b + 1) * 128]),
                    after=(e_trE,), sig="sc")
                e_avmm = p.op("te", lambda eng, ktg=ktg, b=b, st=(kt == 0),
                              sp=(kt == NHB * 4 - 1): eng.matmul(
                    ps_av[:], vtok[:, ktg * DH:(ktg + 1) * DH], etile[b][:],
                    start=st, stop=sp),
                    after=(e_ev, e_vtok_all, e_avs) if kt == 0 else (e_ev,),
                    sig="te" if kt == NHB * 4 - 1 else None)
                e_slot[b] = e_avmm
            e_avmm_prev = e_avmm
            e_avused_prev = p.op("ve", lambda eng: eng.tensor_add(
                avq[:], avq[:], ps_av[:]), after=(e_avmm, e_avs), sig="ve")
            if dbg2 and qt == 0 and half == 0:
                d_er = p.dout("d_erow", (128, T // 4))
                d_avq = p.dout("d_avq", (DH, 128))
                d_t = p.dout("d_t128", (128, 6))
                p.dma("ge", d_er, erow[:], after=(e_exp,))
                p.dma("ge", d_avq, avq[:], after=(e_avused_prev,))
                e_dbg2 = p.dma("ge", d_t, t128[:], after=(e_l,))
                p.op("ve", lambda eng: eng.engine_nop(), after=(e_dbg2,))
                return p

        p.op("ve", lambda eng: eng.reciprocal(t128[:, 4:5], t128[:, 3:4]),
             after=(e_l,))
        e_f = p.op("ve", lambda eng: eng.memset(t128[:, 5:6], 0.0), sig="ve")
        e_tr2 = tr(p, ps_sum[0:1, 0:128], t128[:, 4:5], ident[:],
                   after=(e_f,), sig="te")
        e_rr2 = p.op("sc", lambda eng: eng.copy(rrow[:], ps_sum[0:1, 0:128]),
                     after=(e_tr2,), sig="sc")
        e_ab2 = mm(p, ps_b[0:DH, 0:128], [(kit.onesrow[0:1, 0:DH], rrow[:])],
                   after=(e_rr2,), sig="te")
        e_avq = p.op("ve", lambda eng, qt=qt: eng.tensor_tensor(
            avall[0:DH, qt * 128:(qt + 1) * 128], avq[:], ps_b[0:DH, 0:128],
            ALU.mult), after=(e_ab2, e_avused_prev), sig="ve")
    e_av_all = e_avq

    if dbg:
        d_k = p.dout("d_kT", (DH, T))
        d_q = p.dout("d_qT", (DH, TLOC))
        d_v = p.dout("d_vtok", (128, (T // 128) * DH))
        d_av = p.dout("d_avall", (DH, TLOC))
        p.dma("ge", d_k, kT[:], after=(e_kevac,))
        p.dma("ge", d_q, qTt[:], after=(e_q_all,))
        p.dma("ge", d_v, vtok[:], after=(e_vtok_all,))
        e_dbg = p.dma("ge", d_av, avall[:], after=(e_av_all,))

    e_res_free = None
    for tb in range(TLOC // 512):
        e_ld = None
        for c in range(DC):
            e_ld = p.dma("ge", xres[:, c * 512:(c + 1) * 512],
                         xownH[c * 128:(c + 1) * 128, tb * 512:(tb + 1) * 512],
                         after=(e_res_free, e_q_all) if c == 0 else ())
        e_add = None
        for c in range(DC):
            e_mmo = mm(p, ps_sc[0][:],
                       [(s_wo[0:DH, c * 128:(c + 1) * 128],
                         avall[0:DH, tb * 512:(tb + 1) * 512])],
                       after=(e_av_all, e_add, e_evac_prev[0]), sig="te")
            e_att = p.op("sc", lambda eng, c=c: eng.activation(
                attnb[:], ps_sc[0][:], AF.Identity, bias=s_bo[:, c:c + 1]),
                after=(e_mmo,), sig="sc")
            e_add = p.op("ve", lambda eng, c=c: eng.tensor_add(
                xres[:, c * 512:(c + 1) * 512], xres[:, c * 512:(c + 1) * 512],
                attnb[:]), after=(e_att, e_ld), sig="ve")
        chunks = [xres[:, c * 512:(c + 1) * 512] for c in range(DC)]
        e_ln = emit_ln512(p, kit, chunks, chunks, ps_sum, ps_b, after=(e_add,))
        e_st = None
        for c in range(DC):
            e_st = p.dma("sy", x1T[c * 128:(c + 1) * 128, tb * 512:(tb + 1) * 512],
                         xres[:, c * 512:(c + 1) * 512],
                         after=(e_ln,) if c == 0 else ())
        e_res_free = e_st
    p.op("ve", lambda eng: eng.engine_nop(), after=(e_st,))
    return p



def build_B(dbg=False):
    p = Prog()
    x1T = p.din("x1T", (D, TLOC))
    lng = p.din("lng", (DC, 128, 1))
    lnb = p.din("lnb", (DC, 128, 1))
    abelW = p.din("abelW", (DC, 128, DC, 128))
    abelB = p.din("abelB", (DC, 128, 1))
    gateW = p.din("gateW", (D, 8))
    gateBB = p.din("gateBB", (128, 8))
    w1t = p.din("w1t", (NEXP, 8, 128, DC, 512))
    b1r = p.din("b1r", (NEXP, 128, 32))
    w2t = p.din("w2t", (NEXP, 8, 128, 4, DC, 128))
    b2t = p.din("b2t", (128, NEXP * DC))
    gens = p.din("gens", (3, D, D))
    gensT = p.din("gensT", (3, D, D))
    eye = p.din("eye", (D, D))
    eye128 = p.din("eye128", (128, 128))
    mixw = p.din("mixw", (2 * D, 1))
    mixb = p.din("mixb", (1, 1))
    x2T = p.dout("x2T", (D, TLOC))
    TH = TLOC // 2

    ident = p.sbuf("ident", (128, 128))
    s_lng = p.sbuf("s_lng", (128, DC))
    s_lnb = p.sbuf("s_lnb", (128, DC))
    s_abelB = p.sbuf("s_abelB", (128, DC))
    s_gw = p.sbuf("s_gw", (128, DC * 8))
    s_gbb = p.sbuf("s_gbb", (128, 8))
    s_mixw = p.sbuf("s_mixw", (128, 16))
    s_mixb = p.sbuf("s_mixb", (1, 1))
    s_b1 = p.sbuf("s_b1", (128, 32))
    s_b2 = p.sbuf("s_b2", (128, NEXP * DC))
    abw = p.sbuf("abw", (128, DC * 128))
    mxX = p.sbuf("mxX", (128, DC * D))
    mxR = p.sbuf("mxR", (128, DC * D))
    mxT = p.sbuf("mxT", (128, DC * D))
    mxrot = p.sbuf("mxrot", (128, DC * D))
    eyebuf = p.sbuf("eyebuf", (128, D))
    xn = mxrot
    moe = mxR
    abelT = mxX
    w1b = p.sbuf("w1b", (128, DC * 512))
    w2b = p.sbuf("w2b", (128, 4 * DC * 128))
    resb = w2b
    hgr = p.sbuf("hgr", (128, 4 * 512))
    lgt = p.sbuf("lgt", (128, 8))
    wd = p.sbuf("wd", (128, 16 * 8))
    wrow = p.sbuf("wrow", (1, 512))
    grow = p.sbuf("grow", (1, TLOC))
    gb = p.sbuf("gbuf", (128, 512))
    t2 = top2_scratch(p, "t2B")
    kit = LnKit(p, "lnB", x2s=w1b)

    ps_a = p.psum("ps_a", (128, 512))
    ps_h = [p.psum(f"ps_h{i}", (128, 512)) for i in range(2)]
    ps_eo = [p.psum(f"ps_eo{i}", (128, 512)) for i in range(2)]
    ps_bc = p.psum("ps_bc", (128, 512))
    ps_sum = p.psum("ps_sum", (1, 512))

    e_id = p.dma("sy", ident[:], eye128)
    for c in range(DC):
        p.dma("sy", s_lng[:, c:c + 1], lng[c])
        p.dma("sy", s_lnb[:, c:c + 1], lnb[c])
        p.dma("sy", s_abelB[:, c:c + 1], abelB[c])
        p.dma("sy", s_gw[:, c * 8:(c + 1) * 8], gateW[c * 128:(c + 1) * 128, :])
    p.dma("sy", s_gbb[:], gateBB)
    for c in range(16):
        p.dma("sy", s_mixw[:, c:c + 1], mixw[c * 128:(c + 1) * 128, :])
    p.dma("sy", s_mixb[:], mixb)
    p.dma("sy", s_b2[:], b2t)
    e_const = p.dma("sy", eyebuf[:], eye[0:128, :])

    def mm1024(dst, lhsT_buf, rhs_buf, after=(), scale=1.0, add_eye=False):
        e_last = None
        for r in range(DC):
            for half in range(2):
                pairs = [(lhsT_buf[:, c * D + r * 128: c * D + (r + 1) * 128],
                          rhs_buf[:, c * D + half * 512: c * D + half * 512 + 512])
                         for c in range(DC)]
                e_g = mm(p, ps_a[:], pairs,
                         after=tuple(after) + ((e_last,) if e_last else ()),
                         sig="te")
                if add_eye:
                    e_ey = p.dma("ge", eyebuf[:, half * 512:half * 512 + 512],
                                 eye[r * 128:(r + 1) * 128, half * 512:half * 512 + 512],
                                 after=(e_last,) if e_last else ())
                    e_c = p.op("sc", lambda eng, r=r, half=half, scale=scale:
                               eng.mul(dst[:, r * D + half * 512:
                                           r * D + half * 512 + 512],
                                       ps_a[:], scale), after=(e_g,), sig="sc")
                    e_last = p.op("ve", lambda eng, r=r, half=half: eng.tensor_add(
                        dst[:, r * D + half * 512: r * D + half * 512 + 512],
                        dst[:, r * D + half * 512: r * D + half * 512 + 512],
                        eyebuf[:, half * 512: half * 512 + 512]),
                        after=(e_c, e_ey), sig="ve")
                else:
                    e_last = p.op("sc", lambda eng, r=r, half=half, scale=scale:
                                  eng.mul(dst[:, r * D + half * 512:
                                              r * D + half * 512 + 512],
                                          ps_a[:], scale), after=(e_g,), sig="sc")
        return e_last

    def transpose1024(dst, srcb, after=()):
        e_last = None
        for r in range(DC):
            for c in range(DC):
                e_t = tr(p, ps_a[:, 0:128],
                         srcb[:, c * D + r * 128:c * D + (r + 1) * 128],
                         ident[:],
                         after=tuple(after) + ((e_last,) if e_last else (e_id,)),
                         sig="te")
                e_last = p.op("sc", lambda eng, r=r, c=c: eng.copy(
                    dst[:, r * D + c * 128: r * D + (c + 1) * 128],
                    ps_a[:, 0:128]), after=(e_t,), sig="sc")
        return e_last

    DEG = 14
    e_rot = None
    e_rotL = None
    for gi in (2, 1, 0):
        e_x = None
        for r in range(DC):
            e_x = p.dma("sy", mxX[:, r * D:(r + 1) * D],
                        gens[gi, r * 128:(r + 1) * 128, :],
                        after=(e_rot,) if (gi != 2 and r == 0) else ())
        e_r = e_rot
        for r in range(DC):
            e_l = p.dma("sy", mxR[:, r * D:(r + 1) * D],
                        gensT[gi, r * 128:(r + 1) * 128, :],
                        after=(e_r,) if e_r else ())
            e_s = p.op("sc", lambda eng, r=r: eng.mul(
                mxR[:, r * D:(r + 1) * D], mxR[:, r * D:(r + 1) * D],
                1.0 / DEG), after=(e_l,), sig="sc")
            e_ey = p.dma("ge", eyebuf[:], eye[r * 128:(r + 1) * 128, :],
                         after=(e_r,) if e_r else ())
            e_r = p.op("ve", lambda eng, r=r: eng.tensor_add(
                mxR[:, r * D:(r + 1) * D], mxR[:, r * D:(r + 1) * D],
                eyebuf[:]), after=(e_s, e_ey), sig="ve")
        cur, oth = mxR, mxT
        for k in range(DEG - 1, 0, -1):
            e_r = mm1024(oth, mxX, cur, after=(e_r, e_x), scale=1.0 / k,
                         add_eye=True)
            cur, oth = oth, cur
        assert cur is mxT
        if gi == 2:
            e_rot = p.op("ve", lambda eng: eng.tensor_copy(mxrot[:], mxT[:]),
                         after=(e_r,), sig="ve")
        else:
            e_tp = transpose1024(mxX, mxrot, after=(e_r,))
            e_rot = mm1024(mxrot, mxX, mxT, after=(e_tp,))
    e_rotL = transpose1024(mxT, mxrot, after=(e_rot,))

    e_x2st = None
    kbank_free = {0: None, 1: None}
    for th in range(2):
        toff = th * TH
        e_lx = None
        for c in range(DC):
            e_lx = p.dma("sy", moe[:, c * TH:(c + 1) * TH],
                         x1T[c * 128:(c + 1) * 128, toff:toff + TH],
                         after=(e_x2st, e_rotL) if c == 0 else ())
        e_ln = None
        for blk in range(TH // 512):
            ins = [moe[:, c * TH + blk * 512: c * TH + blk * 512 + 512]
                   for c in range(DC)]
            outs = [xn[:, c * TH + blk * 512: c * TH + blk * 512 + 512]
                    for c in range(DC)]
            e_ln = emit_ln512(p, kit, ins, outs, ps_sum, ps_bc,
                              after=(e_lx, e_ln, e_rotL),
                              gamma=[s_lng[:, c:c + 1] for c in range(DC)],
                              beta=[s_lnb[:, c:c + 1] for c in range(DC)])
        e_wdT = None
        for tt in range(TH // 128):
            e_gl = mm(p, ps_a[:, 0:8],
                      [(xn[:, c * TH + tt * 128: c * TH + (tt + 1) * 128],
                        s_gw[:, c * 8:(c + 1) * 8]) for c in range(DC)],
                      after=(e_ln, e_wdT), sig="te")
            e_lg = p.op("ve", lambda eng: eng.tensor_add(
                lgt[:], ps_a[0:128, 0:8], s_gbb[:]), after=(e_gl,), sig="ve")
            e_wdT = emit_top2(p, t2, lgt[:], wd[:, tt * 8:(tt + 1) * 8], "softmax",
                              after=(e_lg, t2["e_ones"]))
        e_ab = None
        for fc in range(DC):
            e_lw = p.dma("ge", abw[:], abelW[fc].rearrange("p c f -> p (c f)"),
                         after=(e_ab,) if e_ab else ())
            for tb in range(TH // 512):
                e_g = mm(p, ps_h[tb % 2][:],
                         [(abw[:, c * 128:(c + 1) * 128],
                           xn[:, c * TH + tb * 512: c * TH + tb * 512 + 512])
                          for c in range(DC)],
                         after=(e_lw, e_ln, kbank_free[tb % 2]), sig="te")
                e_ab = p.op("sc", lambda eng, fc=fc, tb=tb: eng.activation(
                    abelT[:, fc * TH + tb * 512: fc * TH + tb * 512 + 512],
                    ps_h[tb % 2][:], AF.Gelu, bias=s_abelB[:, fc:fc + 1]),
                    after=(e_g,), sig="sc")
                kbank_free[tb % 2] = e_ab
        e_minit = p.op("ve", lambda eng: eng.memset(moe[:], 0.0),
                       after=(e_ln, e_rotL, e_ab), sig="ve")
        e_h_free = e_ab
        e_w1 = e_w2 = None
        e_w1use = None
        e_w2use = e_x2st
        for e in range(NEXP):
            e_b1l = p.dma("sy", s_b1[:], b1r[e],
                          after=(e_h_free,) if e_h_free else ())
            for fg in range(8):
                e_w1 = p.dma("sy", w1b[:], w1t[e, fg].rearrange("p c f -> p (c f)"),
                             after=(e_w1use,) if e_w1use else (e_ln, e_ab))
                e_w2 = p.dma("sy", w2b[:], w2t[e, fg].rearrange("p a b f -> p (a b f)"),
                             after=(e_w2use,) if e_w2use else ())
                for tb in range(TH // 512):
                    e_tp = None
                    for q4 in range(4):
                        tt = tb * 4 + q4
                        e_tp = tr(p, ps_sum[0:1, q4 * 128:(q4 + 1) * 128],
                                  wd[:, tt * 8 + e: tt * 8 + e + 1], ident[:],
                                  after=(e_wdT, e_h_free) if q4 == 0 else (),
                                  sig="te" if q4 == 3 else None)
                    e_wr = p.op("sc", lambda eng: eng.copy(
                        wrow[0:1, :], ps_sum[0:1, :]), after=(e_tp,), sig="sc")
                    e_bc = mm(p, ps_bc[:], [(kit.onesrow[:], wrow[0:1, :])],
                              after=(e_wr,), sig="te")
                    e_hg = []
                    for f4 in range(4):
                        e_hm = mm(p, ps_h[f4 % 2][:],
                                  [(w1b[:, c * 512 + f4 * 128: c * 512 + (f4 + 1) * 128],
                                    xn[:, c * TH + tb * 512: c * TH + tb * 512 + 512])
                                   for c in range(DC)],
                                  after=(e_w1, e_b1l, e_ln,
                                         e_hg[-2] if len(e_hg) >= 2 else e_h_free),
                                  sig="te")
                        e_w1use = e_hm
                        e_he = p.op("sc", lambda eng, e=e, fg=fg, f4=f4:
                                    eng.activation(hgr[:, f4 * 512:(f4 + 1) * 512],
                                                   ps_h[f4 % 2][:], AF.Gelu,
                                                   bias=s_b1[:, fg * 4 + f4:
                                                             fg * 4 + f4 + 1]),
                                    after=(e_hm,), sig="sc")
                        e_hw = p.op("ve", lambda eng, f4=f4: eng.tensor_mul(
                            hgr[:, f4 * 512:(f4 + 1) * 512],
                            hgr[:, f4 * 512:(f4 + 1) * 512], ps_bc[:]),
                            after=(e_he, e_bc), sig="ve")
                        e_hg.append(e_hw)
                    e_moe = None
                    for dt in range(DC):
                        e_em = mm(p, ps_eo[dt % 2][:],
                                  [(w2b[:, (f4 * DC + dt) * 128:
                                        (f4 * DC + dt + 1) * 128],
                                    hgr[:, f4 * 512:(f4 + 1) * 512])
                                   for f4 in range(4)],
                                  after=(e_w2, e_hg[-1], e_minit, e_moe), sig="te")
                        e_w2use = e_em
                        e_moe = p.op("ve", lambda eng, dt=dt, tb=tb: eng.tensor_add(
                            moe[:, dt * TH + tb * 512: dt * TH + tb * 512 + 512],
                            moe[:, dt * TH + tb * 512: dt * TH + tb * 512 + 512],
                            ps_eo[dt % 2][:]), after=(e_em,), sig="ve")
                    e_h_free = e_moe
        if dbg and th == 0:
            d_xn = p.dout("d_xn", (128, DC * TH))
            d_ab = p.dout("d_ab", (128, DC * TH))
            d_wd = p.dout("d_wd", (128, 128))
            d_moe = p.dout("d_moe", (128, DC * TH))
            d_rot = p.dout("d_rot", (128, DC * D))
            p.dma("ge", d_xn, xn[:], after=(e_ln,))
            p.dma("ge", d_ab, abelT[:], after=(e_ab,))
            p.dma("ge", d_wd, wd[:], after=(e_wdT,))
            p.dma("ge", d_moe, moe[:], after=(e_h_free,))
            p.dma("ge", d_rot, mxT[:], after=(e_rotL,))
        e_na = None
        for j in range(DC):
            for tb in range(TH // 512):
                e_g = mm(p, ps_a[:],
                         [(mxT[:, c * D + j * 128: c * D + (j + 1) * 128],
                           moe[:, c * TH + tb * 512: c * TH + tb * 512 + 512])
                          for c in range(DC)],
                         after=(e_rotL, e_h_free, e_na), sig="te")
                e_na = p.op("sc", lambda eng, j=j, tb=tb: eng.copy(
                    xn[:, j * TH + tb * 512: j * TH + tb * 512 + 512], ps_a[:]),
                    after=(e_g,), sig="sc")
        if dbg and th == 0:
            d_na = p.dout("d_na", (128, DC * TH))
            p.dma("ge", d_na, xn[:], after=(e_na,))
        e_gr = None
        for tb in range(TH // 512):
            pairs = ([(s_mixw[:, c:c + 1],
                       abelT[:, c * TH + tb * 512: c * TH + tb * 512 + 512])
                      for c in range(DC)] +
                     [(s_mixw[:, 8 + c:9 + c],
                       xn[:, c * TH + tb * 512: c * TH + tb * 512 + 512])
                      for c in range(DC)])
            e_g = mm(p, ps_sum[0:1, :], pairs, after=(e_na, e_gr), sig="te")
            e_gr = p.op("sc", lambda eng, tb=tb, toff=toff: eng.activation(
                grow[0:1, toff + tb * 512: toff + tb * 512 + 512], ps_sum[0:1, :],
                AF.Sigmoid, bias=s_mixb[0:1, 0:1]), after=(e_g,), sig="sc")
        if dbg and th == 0:
            d_gr = p.dout("d_gr", (1, TLOC))
            p.dma("ge", d_gr, grow[0:1, :], after=(e_gr,))
        e_st = None
        for tb in range(TH // 512):
            e_lr = None
            for c in range(DC):
                e_lr = p.dma("ge", resb[:, c * 512:(c + 1) * 512],
                             x1T[c * 128:(c + 1) * 128,
                                 toff + tb * 512: toff + tb * 512 + 512],
                             after=(e_st, e_na) if c == 0 else ())
            e_gb = mm(p, ps_bc[:],
                      [(kit.onesrow[:],
                        grow[0:1, toff + tb * 512: toff + tb * 512 + 512])],
                      after=(e_gr, e_h_free, e_st), sig="te")
            e_cmb = None
            for c in range(DC):
                ab = abelT[:, c * TH + tb * 512: c * TH + tb * 512 + 512]
                na = xn[:, c * TH + tb * 512: c * TH + tb * 512 + 512]
                rs = resb[:, c * 512:(c + 1) * 512]
                p.op("ve", lambda eng, rs=rs: eng.tensor_add(rs, rs, rs),
                     after=(e_lr,) if c == 0 else ())
                p.op("ve", lambda eng, na=na, ab=ab: eng.tensor_sub(gb[:], na, ab),
                     after=(e_na,) if c == 0 else ())
                p.op("ve", lambda eng: eng.tensor_mul(gb[:], gb[:], ps_bc[:]),
                     after=(e_gb,) if c == 0 else ())
                p.op("ve", lambda eng, rs=rs, ab=ab: eng.tensor_add(rs, rs, ab))
                e_cmb = p.op("ve", lambda eng, rs=rs: eng.tensor_add(rs, rs, gb[:]),
                             sig="ve" if c == DC - 1 else None)
            chunks = [resb[:, c * 512:(c + 1) * 512] for c in range(DC)]
            e_ln2 = emit_ln512(p, kit, chunks, chunks, ps_sum, ps_bc,
                               after=(e_cmb,))
            for c in range(DC):
                e_st = p.dma("sy", x2T[c * 128:(c + 1) * 128,
                                       toff + tb * 512: toff + tb * 512 + 512],
                             resb[:, c * 512:(c + 1) * 512],
                             after=(e_ln2,) if c == 0 else ())
        e_x2st = e_st
    p.op("ve", lambda eng: eng.engine_nop(), after=(e_x2st,))
    return p



def build_C1():
    p = Prog()
    x2T = p.din("x2T", (D, T))
    winxi = p.din("winxi", (2, DC, 128, 128))
    winz = p.din("winz", (2, DC, 128, 128))
    convw = p.din("convw", (2, 128, 4))
    convb = p.din("convb", (2, 128, 1))
    wxsl = p.din("wxsl", (2, 128, 96))
    xcT = p.dout("xcT", (CHL, T))
    szT = p.dout("szT", (CHL, T))
    xdp = p.dout("xdp", (96, T))

    s_wxi = p.sbuf("s_wxi", (128, 2 * DC * 128))
    s_wz = p.sbuf("s_wz", (128, 2 * DC * 128))
    s_cw = p.sbuf("s_cw", (128, 2 * 4))
    s_cb = p.sbuf("s_cb", (128, 2))
    s_wx = p.sbuf("s_wx", (128, 2 * 96))
    xst = [p.sbuf(f"xst{i}", (128, DC * 512)) for i in range(2)]
    xib = p.sbuf("xib", (128, 2 * 515))
    xcb = p.sbuf("xcb", (128, 2 * 512))
    szb = p.sbuf("szb", (128, 2 * 512))
    xdb = p.sbuf("xdb", (96, 512))

    ps_x = [p.psum(f"ps_x{i}", (128, 512)) for i in range(2)]
    ps_d = p.psum("ps_d", (96, 512))

    e_w = None
    for ch in range(2):
        for c in range(DC):
            p.dma("sy", s_wxi[:, (ch * DC + c) * 128:(ch * DC + c + 1) * 128],
                  winxi[ch, c])
            e_w = p.dma("sy", s_wz[:, (ch * DC + c) * 128:(ch * DC + c + 1) * 128],
                        winz[ch, c])
        p.dma("sy", s_cw[:, ch * 4:(ch + 1) * 4], convw[ch])
        p.dma("sy", s_cb[:, ch:ch + 1], convb[ch])
        e_w = p.dma("sy", s_wx[:, ch * 96:(ch + 1) * 96], wxsl[ch])
    e_halo = p.op("ve", lambda eng: eng.memset(xib[:], 0.0), sig="ve")

    NTB = T // 512
    e_stage_free = [None, None]
    e_conv_done = None
    e_z_done = None
    e_xc_st = e_sz_st = e_xd_st = None
    for tb in range(NTB):
        buf = tb % 2
        e_ld = None
        for c in range(DC):
            e_ld = p.dma("sy", xst[buf][:, c * 512:(c + 1) * 512],
                         x2T[c * 128:(c + 1) * 128, tb * 512:(tb + 1) * 512],
                         after=(e_stage_free[buf],) if c == 0 else ())
        e_xi = {}
        for ch in range(2):
            e_m = mm(p, ps_x[ch][:],
                     [(s_wxi[:, (ch * DC + c) * 128:(ch * DC + c + 1) * 128],
                       xst[buf][:, c * 512:(c + 1) * 512]) for c in range(DC)],
                     after=(e_ld, e_w, e_conv_done), sig="te")
            e_xi[ch] = p.op("sc", lambda eng, ch=ch: eng.copy(
                xib[:, ch * 515 + 3: ch * 515 + 515], ps_x[ch][:]),
                after=(e_m, e_halo), sig="sc")
        e_cv = None
        for ch in range(2):
            o = ch * 515
            e1 = p.op("ve", lambda eng, ch=ch, o=o: eng.tensor_scalar(
                xcb[:, ch * 512:(ch + 1) * 512], xib[:, o:o + 512],
                s_cw[:, ch * 4:ch * 4 + 1], None, ALU.mult), after=(e_xi[ch],))
            for j in range(1, 4):
                e1 = p.op("ve", lambda eng, ch=ch, o=o, j=j: eng.scalar_tensor_tensor(
                    xcb[:, ch * 512:(ch + 1) * 512], xib[:, o + j:o + j + 512],
                    s_cw[:, ch * 4 + j:ch * 4 + j + 1],
                    xcb[:, ch * 512:(ch + 1) * 512], ALU.mult, ALU.add))
            e_cv = p.op("ve", lambda eng, o=o: eng.tensor_copy(
                xib[:, o:o + 3], xib[:, o + 512:o + 515]),
                sig="ve" if ch == 1 else None)
        e_conv_done = e_cv
        e_si = None
        for ch in range(2):
            e_si = p.op("sc", lambda eng, ch=ch: eng.activation(
                xcb[:, ch * 512:(ch + 1) * 512], xcb[:, ch * 512:(ch + 1) * 512],
                AF.Silu, bias=s_cb[:, ch:ch + 1]), after=(e_cv, e_xc_st), sig="sc")
        for ch in range(2):
            e_xc_st = p.dma("sy", xcT[ch * 128:(ch + 1) * 128, tb * 512:(tb + 1) * 512],
                            xcb[:, ch * 512:(ch + 1) * 512], after=(e_si,))
        for ch in range(2):
            e_m = mm(p, ps_x[ch][:],
                     [(s_wz[:, (ch * DC + c) * 128:(ch * DC + c + 1) * 128],
                       xst[buf][:, c * 512:(c + 1) * 512]) for c in range(DC)],
                     after=(e_xi[ch], e_z_done), sig="te")
            e_z = p.op("sc", lambda eng, ch=ch: eng.activation(
                szb[:, ch * 512:(ch + 1) * 512], ps_x[ch][:], AF.Silu),
                after=(e_m, e_sz_st), sig="sc")
            e_sz_st = p.dma("sy", szT[ch * 128:(ch + 1) * 128, tb * 512:(tb + 1) * 512],
                            szb[:, ch * 512:(ch + 1) * 512], after=(e_z,))
            e_z_done = e_z
        e_stage_free[buf] = e_z_done
        e_dm = mm(p, ps_d[:],
                  [(s_wx[:, ch * 96:(ch + 1) * 96], xcb[:, ch * 512:(ch + 1) * 512])
                   for ch in range(2)], after=(e_si, e_xd_st), sig="te")
        e_de = p.op("sc", lambda eng: eng.copy(xdb[:], ps_d[:]), after=(e_dm,),
                    sig="sc")
        e_xd_st = p.dma("sy", xdp[:, tb * 512:(tb + 1) * 512], xdb[:],
                        after=(e_de,))
    p.op("ve", lambda eng: eng.engine_nop(), after=(e_xc_st, e_sz_st, e_xd_st))
    return p



def build_C2(dbg=0):
    p = Prog()
    xcT = p.din("xcT", (CHL, T))
    szT = p.din("szT", (CHL, T))
    xdall = p.din("xdall", (NCORE, 96, T))
    wdtsl = p.din("wdtsl", (2, DTR, 128))
    bdtsl = p.din("bdtsl", (2, 128, 1))
    alogsl = p.din("alogsl", (2, 128, 16))
    dsl = p.din("dsl", (2, 128, 1))
    woutsl = p.din("woutsl", (2, DC, 128, 128))
    selbc = p.din("selbc", (96, 2 * DST * 128))
    wop = p.dout("wop", (D, T))

    TBS = 2048
    s_wdt = p.sbuf("s_wdt", (DTR, 2 * 128))
    s_bdt = p.sbuf("s_bdt", (128, 2))
    s_dd = p.sbuf("s_dd", (128, 2))
    s_an = p.sbuf("s_an", (128, 2 * DST))
    s_wo = p.sbuf("s_wo", (128, 2 * DC * 128))
    xdsum = p.nc.dram_tensor("xdsum", [96, T], F32, kind="Internal").ap()
    xdsb = p.sbuf("xdsb", (96, TBS))
    onesr = p.sbuf("onesr", (1, 128))
    s_sel = p.sbuf("s_sel", (96, 2 * DST * 128))
    xcb = p.sbuf("xcb", (128, 2 * TBS))
    szb = p.sbuf("szb", (128, 2 * TBS))
    xstg = xcb
    xstg2 = szb
    dlt = p.sbuf("dlt", (128, 2 * TBS))
    ub = p.sbuf("ub", (128, 2 * TBS))
    bb = p.sbuf("bb", (128, TBS))
    cb = p.sbuf("cb", (128, TBS))
    dab = p.sbuf("dab", (128, TBS))
    dbu = p.sbuf("dbu", (128, TBS))
    hb = p.sbuf("hb", (128, TBS))
    yac = p.sbuf("yac", (128, 2 * TBS))
    carry = p.sbuf("carry", (128, 2 * DST))
    one1 = p.sbuf("one1C", (128, 1))
    ps_a = [p.psum(f"ps_a{i}", (128, 512)) for i in range(2)]
    ps_w = [p.psum(f"ps_w{i}", (128, 512)) for i in range(2)]

    e_c = None
    for ch in range(2):
        p.dma("sy", s_wdt[:, ch * 128:(ch + 1) * 128], wdtsl[ch])
        p.dma("sy", s_bdt[:, ch:ch + 1], bdtsl[ch])
        p.dma("sy", s_dd[:, ch:ch + 1], dsl[ch])
        p.dma("sy", s_an[:, ch * DST:(ch + 1) * DST], alogsl[ch])
        for dt in range(DC):
            e_c = p.dma("sy", s_wo[:, (ch * DC + dt) * 128:(ch * DC + dt + 1) * 128],
                        woutsl[ch, dt])
    e_ones = p.op("ve", lambda eng: eng.memset(onesr[:], 1.0), sig="ve")
    p.op("ve", lambda eng: eng.memset(one1[:], 1.0))
    e_sel = p.dma("sy", s_sel[:], selbc)
    e_zc = p.op("ve", lambda eng: eng.memset(carry[:], 0.0), sig="ve")
    e_an = p.op("sc", lambda eng: eng.activation(s_an[:], s_an[:], AF.Exp),
                after=(e_c,), sig="sc")
    e_an = p.op("ve", lambda eng: eng.tensor_scalar(
        s_an[:], s_an[:], -1.0, None, ALU.mult), after=(e_an,), sig="ve")

    if dbg == 4:
        p.op("ve", lambda eng: eng.engine_nop(), after=(e_an, e_sel, e_c))
        return p
    e_sum_st = None
    for q in range(4):
        e_a = p.dma("sy", xstg[0:96, 0:4096], xdall[0, :, q * 4096:(q + 1) * 4096],
                    after=(e_sum_st,) if e_sum_st else ())
        e_acc = None
        for r in range(1, NCORE):
            e_l = p.dma("sy", xstg2[0:96, 0:4096], xdall[r, :, q * 4096:(q + 1) * 4096],
                        after=(e_acc,) if e_acc else ())
            e_acc = p.op("ve", lambda eng: eng.tensor_add(
                xstg[0:96, 0:4096], xstg[0:96, 0:4096], xstg2[0:96, 0:4096]),
                after=(e_l, e_a), sig="ve")
        e_sum_st = p.dma("sy", xdsum[:, q * 4096:(q + 1) * 4096], xstg[0:96, 0:4096],
                         after=(e_acc,))
    e_xds_hbm = e_sum_st

    if dbg == 1:
        d_xs = p.dout("d_xds", (96, T))
        e_d = None
        for q in range(8):
            e_l = p.dma("sy", xdsb[0:96, 0:2048], xdsum[:, q * 2048:(q + 1) * 2048],
                        after=(e_xds_hbm, e_d))
            e_d = p.dma("ge", d_xs[:, q * 2048:(q + 1) * 2048], xdsb[0:96, 0:2048],
                        after=(e_l,))
        p.op("ve", lambda eng: eng.engine_nop(), after=(e_d,))
        return p
    NTB = T // TBS
    e_wo_st = None
    e_xc_free = None
    e_xds_free = None
    e_wo_ev_last = None
    for tb in range(NTB):
        t0 = tb * TBS
        e_lxc = None
        e_lsz = None
        for ch in range(2):
            e_lxc = p.dma("sy", xcb[:, ch * TBS:(ch + 1) * TBS],
                          xcT[ch * 128:(ch + 1) * 128, t0:t0 + TBS],
                          after=(e_xc_free, e_xds_hbm) if ch == 0 else ())
            e_lsz = p.dma("sy", szb[:, ch * TBS:(ch + 1) * TBS],
                          szT[ch * 128:(ch + 1) * 128, t0:t0 + TBS],
                          after=(e_xc_free,) if ch == 0 else ())
        e_xds = p.dma("sy", xdsb[:], xdsum[:, t0:t0 + TBS],
                      after=(e_xds_free,) if e_xds_free else ())
        e_dl = {}
        e_pa_free = [None, None]
        for ch in range(2):
            e_d = None
            for blk in range(TBS // 512):
                e_m = mm(p, ps_a[blk % 2][:],
                         [(s_wdt[0:DTR, ch * 128:(ch + 1) * 128],
                           xdsb[0:DTR, blk * 512: blk * 512 + 512])],
                         after=(e_xds, e_pa_free[blk % 2]), sig="te")
                e_e = p.op("sc", lambda eng, ch=ch, blk=blk: eng.activation(
                    dlt[:, ch * TBS + blk * 512: ch * TBS + blk * 512 + 512],
                    ps_a[blk % 2][:], AF.Exp, bias=s_bdt[:, ch:ch + 1]),
                    after=(e_m,), sig="sc")
                e_pa_free[blk % 2] = e_e
                e_d = p.op("sc", lambda eng, ch=ch, blk=blk: eng.activation(
                    dlt[:, ch * TBS + blk * 512: ch * TBS + blk * 512 + 512],
                    dlt[:, ch * TBS + blk * 512: ch * TBS + blk * 512 + 512],
                    AF.Ln, bias=one1[:, 0:1]), sig="sc")
            e_dl[ch] = e_d
        if dbg == 2 and tb == 0:
            d_dl = p.dout("d_dlt", (128, 2 * TBS))
            e_d = p.dma("ge", d_dl, dlt[:], after=(e_dl[0], e_dl[1]))
            p.op("ve", lambda eng: eng.engine_nop(), after=(e_d,))
            return p
        e_u = None
        for ch in range(2):
            e_u = p.op("ve", lambda eng, ch=ch: eng.tensor_mul(
                ub[:, ch * TBS:(ch + 1) * TBS], dlt[:, ch * TBS:(ch + 1) * TBS],
                xcb[:, ch * TBS:(ch + 1) * TBS]),
                after=(e_dl[ch], e_lxc), sig="ve" if ch == 1 else None)
        e_y0 = p.op("ve", lambda eng: eng.memset(yac[:], 0.0), after=(e_wo_st,),
                    sig="ve")
        e_db_prev = None
        e_hc_prev = None
        e_h_prev = e_wo_st
        e_y_prev = None
        e_pw_free = [e_wo_ev_last, e_wo_ev_last]
        for n in range(DST):
            for blk in range(TBS // 512):
                e_b = mm(p, ps_w[0][:],
                         [(s_sel[:, n * 128:(n + 1) * 128],
                           xdsb[:, blk * 512: blk * 512 + 512])],
                         after=(e_xds, e_sel, e_pw_free[0], e_db_prev), sig="te")
                e_be = p.op("sc", lambda eng, blk=blk: eng.copy(
                    bb[:, blk * 512:(blk + 1) * 512], ps_w[0][:]),
                    after=(e_b,), sig="sc")
                e_pw_free[0] = e_be
                e_cm = mm(p, ps_w[1][:],
                          [(s_sel[:, (DST + n) * 128:(DST + n + 1) * 128],
                            xdsb[:, blk * 512: blk * 512 + 512])],
                          after=(e_pw_free[1], e_hc_prev), sig="te")
                e_ce = p.op("sc", lambda eng, blk=blk: eng.copy(
                    cb[:, blk * 512:(blk + 1) * 512], ps_w[1][:]),
                    after=(e_cm,), sig="sc")
                e_pw_free[1] = e_ce
            for ch in range(2):
                e_da = p.op("sc", lambda eng, ch=ch, n=n: eng.activation(
                    dab[:], dlt[:, ch * TBS:(ch + 1) * TBS], AF.Exp,
                    scale=s_an[:, ch * DST + n: ch * DST + n + 1]),
                    after=(e_an, e_dl[ch], e_h_prev), sig="sc")
                e_db = p.op("ve", lambda eng, ch=ch: eng.tensor_mul(
                    dbu[:], ub[:, ch * TBS:(ch + 1) * TBS], bb[:]),
                    after=(e_u, e_be, e_h_prev), sig="ve")
                e_db_prev = e_db
                e_h = p.op("ve", lambda eng, ch=ch, n=n: eng.tensor_tensor_scan(
                    hb[:], dab[:], dbu[:],
                    carry[:, ch * DST + n: ch * DST + n + 1], ALU.mult, ALU.add),
                    after=(e_da, e_zc, e_y_prev), sig="ve")
                e_h_prev = e_h
                p.op("ve", lambda eng, ch=ch, n=n: eng.tensor_copy(
                    carry[:, ch * DST + n: ch * DST + n + 1], hb[:, TBS - 1:TBS]),
                    after=(e_h,))
                e_hc_prev = p.op("ve", lambda eng: eng.tensor_mul(
                    hb[:], hb[:], cb[:]), after=(e_ce,), sig="ve")
                e_y = p.op("ve", lambda eng, ch=ch: eng.tensor_add(
                    yac[:, ch * TBS:(ch + 1) * TBS],
                    yac[:, ch * TBS:(ch + 1) * TBS], hb[:]),
                    after=(e_y0,), sig="ve")
                e_y_prev = e_y
            e_bcf = [e_y, e_y]
        e_yn = e_y
        if dbg == 3 and tb == 0:
            d_y = p.dout("d_yac", (128, 2 * TBS))
            e_d = p.dma("ge", d_y, yac[:], after=(e_yn,))
            p.op("ve", lambda eng: eng.engine_nop(), after=(e_d,))
            return p
        e_ya = None
        for ch in range(2):
            p.op("ve", lambda eng, ch=ch: eng.scalar_tensor_tensor(
                yac[:, ch * TBS:(ch + 1) * TBS], xcb[:, ch * TBS:(ch + 1) * TBS],
                s_dd[:, ch:ch + 1], yac[:, ch * TBS:(ch + 1) * TBS],
                ALU.mult, ALU.add), after=(e_yn,) if ch == 0 else ())
            e_ya = p.op("ve", lambda eng, ch=ch: eng.tensor_mul(
                yac[:, ch * TBS:(ch + 1) * TBS], yac[:, ch * TBS:(ch + 1) * TBS],
                szb[:, ch * TBS:(ch + 1) * TBS]),
                after=(e_lsz,) if ch == 0 else (), sig="ve" if ch == 1 else None)
        e_xc_free = e_ya
        e_xds_free = e_yn
        e_woev = None
        for dt in range(DC):
            for blk in range(TBS // 512):
                e_m = mm(p, ps_w[blk % 2][:],
                         [(s_wo[:, (ch * DC + dt) * 128:(ch * DC + dt + 1) * 128],
                           yac[:, ch * TBS + blk * 512: ch * TBS + blk * 512 + 512])
                          for ch in range(2)],
                         after=(e_ya, e_c, e_woev) + ((e_wo_st,) if blk == 0 else ()),
                         sig="te")
                e_woev = p.op("sc", lambda eng, dt=dt, blk=blk: eng.copy(
                    dab[:, blk * 512:(blk + 1) * 512],
                    ps_w[blk % 2][:]), after=(e_m,), sig="sc")
                e_wo_ev_last = e_woev
            e_wo_st = p.dma("sy", wop[dt * 128:(dt + 1) * 128, t0:t0 + TBS],
                            dab[:, 0:TBS], after=(e_woev,))
    p.op("ve", lambda eng: eng.engine_nop(), after=(e_wo_st,))
    return p



def build_D(dbg=0):
    p = Prog()
    resT = p.din("resT", (D, TLOC))
    wps = p.din("wps", (NCORE, D, TLOC))
    router = p.din("router", (D, 8))
    egt = p.din("egt", (NEXP, 128, DC, 4, 128))
    eut = p.din("eut", (NEXP, 128, DC, 4, 128))
    edt = p.din("edt", (NEXP, DC, 128, 4, 128))
    sgt = p.din("sgt", (DC, 128, DC, 128))
    sut = p.din("sut", (DC, 128, DC, 128))
    sdt = p.din("sdt", (DC, 128, DC, 128))
    eye128 = p.din("eye128", (128, 128))
    yout = p.dout("yout", (TLOC, D))

    ident = p.sbuf("ident", (128, 128))
    s_rt = p.sbuf("s_rt", (128, DC * 8))
    x3 = p.sbuf("x3", (128, DC * TLOC))
    acc = p.sbuf("acc", (128, DC * TLOC))
    wbuf = p.sbuf("wbuf", (128, DC * 512))
    wbuf2 = p.sbuf("wbuf2", (128, DC * 512))
    wbufd = p.sbuf("wbufd", (128, 4 * 128))
    act1 = p.sbuf("act1", (128, 4 * 512))
    act2 = p.sbuf("act2", (128, 512))
    shs = p.sbuf("shs", (128, DC * 512))
    lgt = p.sbuf("lgtD", (128, 8))
    cf = p.sbuf("cf", (128, 16 * 8))
    crow = p.sbuf("crow", (1, 512))
    cbb = p.sbuf("cbb", (128, 512))
    mcol = p.sbuf("mcol", (128, 2))
    otile = p.sbuf("otile", (128, 1024))
    t2 = top2_scratch(p, "t2D")
    kit = LnKit(p, "lnD", x2s=shs)

    ps_a = [p.psum(f"ps_a{i}", (128, 512)) for i in range(2)]
    ps_b = p.psum("ps_bD", (128, 512))
    ps_sum = p.psum("ps_sumD", (1, 512))
    ps_t = p.psum("ps_tD", (128, 512))

    e_id = p.dma("sy", ident[:], eye128)
    for c in range(DC):
        p.dma("sy", s_rt[:, c * 8:(c + 1) * 8], router[c * 128:(c + 1) * 128, :])

    e_l0 = None
    for c in range(DC):
        e_l0 = p.dma("sy", x3[:, c * TLOC:(c + 1) * TLOC],
                     resT[c * 128:(c + 1) * 128, :])
    e_ac = None
    for r in range(NCORE):
        for c in range(DC):
            e_l = p.dma("sy" if r % 2 == 0 else "ge",
                        acc[:, c * TLOC:(c + 1) * TLOC],
                        wps[r, c * 128:(c + 1) * 128, :],
                        after=(e_ac,) if e_ac else ())
            e_ac = p.op("ve", lambda eng, c=c: eng.tensor_add(
                x3[:, c * TLOC:(c + 1) * TLOC], x3[:, c * TLOC:(c + 1) * TLOC],
                acc[:, c * TLOC:(c + 1) * TLOC]), after=(e_l, e_l0), sig="ve")
    e_ln = None
    for blk in range(TLOC // 512):
        chunks = [x3[:, c * TLOC + blk * 512: c * TLOC + blk * 512 + 512]
                  for c in range(DC)]
        e_ln = emit_ln512(p, kit, chunks, chunks, ps_sum, ps_b,
                          after=(e_ac, e_ln))

    if dbg == 1:
        d_x3 = p.dout("d_x3", (128, DC * TLOC))
        e_d = p.dma("ge", d_x3, x3[:], after=(e_ln,))
        p.op("ve", lambda eng: eng.engine_nop(), after=(e_d,))
        return p
    e_cfT = None
    for tt in range(TLOC // 128):
        e_gl = mm(p, ps_a[0][:, 0:8],
                  [(x3[:, c * TLOC + tt * 128: c * TLOC + (tt + 1) * 128],
                    s_rt[:, c * 8:(c + 1) * 8]) for c in range(DC)],
                  after=(e_ln, e_cfT), sig="te")
        e_lg = p.op("ve", lambda eng: eng.tensor_copy(lgt[:], ps_a[0][0:128, 0:8]),
                    after=(e_gl,), sig="ve")
        e_rm = p.op("ve", lambda eng: eng.reduce_max(
            mcol[:, 0:1], lgt[:], AX.X), sig="ve")
        e_mn = p.op("ve", lambda eng: eng.tensor_scalar(
            mcol[:, 1:2], mcol[:, 0:1], -1.0, None, ALU.mult),
            after=(e_rm,), sig="ve")
        e_ex = p.op("sc", lambda eng: eng.activation(
            lgt[:], lgt[:], AF.Exp, bias=mcol[:, 1:2],
            accum_out=mcol[:, 0:1]), after=(e_mn,), sig="sc")
        e_rc = p.op("ve", lambda eng: eng.reciprocal(mcol[:, 1:2], mcol[:, 0:1]),
                    after=(e_ex,), sig="ve")
        e_pr = p.op("ve", lambda eng: eng.tensor_scalar(
            lgt[:], lgt[:], mcol[:, 1:2], None, ALU.mult), after=(e_rc,), sig="ve")
        e_cfT = emit_top2(p, t2, lgt[:], cf[:, tt * 8:(tt + 1) * 8], "probs",
                          after=(e_pr, t2["e_ones"]))

    if dbg == 2:
        d_cf = p.dout("d_cf", (128, 16 * 8))
        e_d = p.dma("ge", d_cf, cf[:], after=(e_cfT,))
        p.op("ve", lambda eng: eng.engine_nop(), after=(e_d,))
        return p
    e_sha = None
    e_wl = None
    for tb in range(TLOC // 512):
        tsl = lambda c: x3[:, c * TLOC + tb * 512: c * TLOC + tb * 512 + 512]
        e_tile = None
        for fc in range(DC):
            e_w1 = p.dma("sy", wbuf[:, 0:1024], sgt[fc].rearrange("p c f -> p (c f)"),
                         after=(e_wl,) if e_wl else ())
            e_w2 = p.dma("sy", wbuf2[:, 0:1024], sut[fc].rearrange("p c f -> p (c f)"))
            e_mg = mm(p, ps_a[0][:],
                      [(wbuf[:, c * 128:(c + 1) * 128], tsl(c)) for c in range(DC)],
                      after=(e_w1, e_ln, e_tile, e_cfT, e_sha), sig="te")
            e_sg = p.op("sc", lambda eng: eng.activation(act2[:], ps_a[0][:], AF.Silu),
                        after=(e_mg,), sig="sc")
            e_mu = mm(p, ps_a[1][:],
                      [(wbuf2[:, c * 128:(c + 1) * 128], tsl(c)) for c in range(DC)],
                      after=(e_w2, e_sg), sig="te")
            e_tile = p.op("ve", lambda eng, fc=fc: eng.tensor_mul(
                shs[:, fc * 512:(fc + 1) * 512], ps_a[1][:], act2[:]),
                after=(e_mu,), sig="ve")
            e_wl = e_mu
        for dt in range(DC):
            e_w3 = p.dma("sy", wbuf[:, 0:1024],
                         sdt[dt].rearrange("p c f -> p (c f)"), after=(e_wl,))
            e_md = mm(p, ps_a[0][:],
                      [(wbuf[:, c * 128:(c + 1) * 128],
                        shs[:, c * 512:(c + 1) * 512]) for c in range(DC)],
                      after=(e_w3, e_tile, e_sha), sig="te")
            e_sha = p.op("ve", lambda eng, dt=dt, tb=tb: eng.tensor_add(
                acc[:, dt * TLOC + tb * 512: dt * TLOC + tb * 512 + 512],
                x3[:, dt * TLOC + tb * 512: dt * TLOC + tb * 512 + 512],
                ps_a[0][:]), after=(e_md,), sig="ve")
            e_wl = e_md

    if dbg == 3:
        d_acc = p.dout("d_acc", (128, DC * TLOC))
        e_d = p.dma("ge", d_acc, acc[:], after=(e_sha,))
        p.op("ve", lambda eng: eng.engine_nop(), after=(e_d,))
        return p
    e_free = None
    for e in range(1 if dbg in (6, 7, 9) else NEXP):
        e_lg1 = p.dma("sy", wbuf[:], egt[e].rearrange("p c f q -> p (c f q)"),
                      after=(e_free,) if e_free else (e_sha,))
        e_lu1 = p.dma("sy", wbuf2[:], eut[e].rearrange("p c f q -> p (c f q)"))
        for tb in range(TLOC // 512):
            tsl = lambda c: x3[:, c * TLOC + tb * 512: c * TLOC + tb * 512 + 512]
            e_tp = None
            for q4 in range(4):
                tt = tb * 4 + q4
                e_tp = tr(p, ps_sum[0:1, q4 * 128:(q4 + 1) * 128],
                          cf[:, tt * 8 + e: tt * 8 + e + 1], ident[:],
                          after=(e_cfT, e_free, e_id) if q4 == 0 else (),
                          sig="te" if q4 == 3 else None)
            e_cr = p.op("sc", lambda eng: eng.copy(crow[0:1, :], ps_sum[0:1, :]),
                        after=(e_tp,), sig="sc")
            e_bc0 = mm(p, ps_b[:], [(kit.onesrow[:], crow[0:1, :])],
                       after=(e_cr,), sig="te")
            e_bc = p.op("sc", lambda eng: eng.copy(cbb[:], ps_b[:]),
                        after=(e_bc0,), sig="sc")
            e_a1 = None
            for ft in range(4):
                e_mg = mm(p, ps_a[ft % 2][:],
                          [(wbuf[:, (c * 4 + ft) * 128:(c * 4 + ft + 1) * 128],
                            tsl(c)) for c in range(DC)],
                          after=(e_lg1, e_ln, e_a1, e_sha), sig="te")
                e_sg = p.op("sc", lambda eng, ft=ft: eng.activation(
                    act1[:, ft * 512:(ft + 1) * 512], ps_a[ft % 2][:], AF.Silu),
                    after=(e_mg,), sig="sc")
                e_mu = mm(p, ps_a[ft % 2][:],
                          [(wbuf2[:, (c * 4 + ft) * 128:(c * 4 + ft + 1) * 128],
                            tsl(c)) for c in range(DC)],
                          after=(e_lu1, e_sg), sig="te")
                e_a1 = p.op("ve", lambda eng, ft=ft: eng.tensor_mul(
                    act1[:, ft * 512:(ft + 1) * 512], ps_a[ft % 2][:],
                    act1[:, ft * 512:(ft + 1) * 512]), after=(e_mu,), sig="ve")
            e_dacc = e_free
            for dt in range(DC):
                e_ld = p.dma("ge", wbufd[:], edt[e, dt].rearrange("p f q -> p (f q)"),
                             after=(e_dacc,))
                e_md = mm(p, ps_a[0][:],
                          [(wbufd[:, ft * 128:(ft + 1) * 128],
                            act1[:, ft * 512:(ft + 1) * 512]) for ft in range(4)],
                          after=(e_ld, e_a1, e_dacc), sig="te")
                e_mlt = p.op("ve", lambda eng: eng.tensor_mul(
                    act2[:], ps_a[0][:], cbb[:]), after=(e_md, e_bc),
                    sig="ve" if dbg == 9 else None)
                if dbg == 9 and e == 0 and tb == 3 and dt == 7:
                    d_act2 = p.dout("d_act2", (128, 512))
                    p.dma("sy", d_act2, act2[:], after=(e_mlt,))
                e_dacc = p.op("ve", lambda eng, dt=dt, tb=tb: eng.tensor_add(
                    acc[:, dt * TLOC + tb * 512: dt * TLOC + tb * 512 + 512],
                    acc[:, dt * TLOC + tb * 512: dt * TLOC + tb * 512 + 512],
                    act2[:]), after=(e_sha,), sig="ve")
            e_free = e_dacc

    if dbg in (4, 6, 9):
        d_acc = p.dout("d_acc", (128, DC * TLOC))
        e_f1 = p.op("ve", lambda eng: eng.memset(act1[:], 0.0), after=(e_free,),
                    sig="ve")
        e_f1 = p.op("ve", lambda eng: eng.memset(act1[:], 1.0), sig="ve")
        e_f1 = p.op("ve", lambda eng: eng.memset(act1[:], 2.0), sig="ve")
        e_d = p.dma("sy", d_acc, acc[:], after=(e_f1,))
        p.op("ve", lambda eng: eng.engine_nop(), after=(e_d,))
        return p
    e_st = None
    e_lnf = None
    for blk in range(TLOC // 512):
        chunks = [acc[:, c * TLOC + blk * 512: c * TLOC + blk * 512 + 512]
                  for c in range(DC)]
        e_lnf = emit_ln512(p, kit, chunks, chunks, ps_sum, ps_b,
                           after=(e_free, e_lnf))
        for q in range(4):
            tt = blk * 4 + q
            e_tp = None
            e_ev = None
            for c in range(DC):
                e_tp = tr(p, ps_t[:, (c % 4) * 128:(c % 4) * 128 + 128],
                          acc[:, c * TLOC + tt * 128: c * TLOC + (tt + 1) * 128],
                          ident[:],
                          after=(e_lnf, e_st) if c == 0 else
                                ((e_ev,) if c == 4 else ()),
                          sig="te" if c % 4 == 3 else None)
                if c % 4 == 3:
                    e_ev = p.op("sc", lambda eng, c=c: eng.copy(
                        otile[:, (c - 3) * 128:(c + 1) * 128], ps_t[:, :]),
                        after=(e_tp,), sig="sc")
            e_st = p.dma("sy", yout[tt * 128:(tt + 1) * 128, :], otile[:],
                         after=(e_ev,))
    p.op("ve", lambda eng: eng.engine_nop(), after=(e_st,))
    return p



def _f(a):
    return np.ascontiguousarray(a, dtype=np.float32)


def kernel(**inp):
    x = _f(inp["L0_attn_x"])[0]
    xT = _f(x.T)

    eye128 = np.eye(128, dtype=np.float32)
    eyeD = np.eye(D, dtype=np.float32)
    iotaf = np.broadcast_to(np.arange(512, dtype=np.float32), (128, 512)).copy()

    pa = build_A()
    nca = pa.build()
    in_maps = []
    for c in range(NCORE):
        qgc = (2048 * c + 128 * np.arange(16)[None, :] +
               np.arange(128)[:, None]).astype(np.float32)
        qgkbc = (qgc[:, :, None] - 512.0 * np.arange(32, dtype=np.float32)[None, None, :]
                 ).reshape(128, 512).astype(np.float32)
        in_maps.append(dict(
            xT=xT, xownT=_f(xT[:, c * TLOC:(c + 1) * TLOC]), qgkb=qgkbc,
            wqT=_f(inp["L0_attn_Wq"].T), wkT=_f(inp["L0_attn_Wk"].T),
            wvT=_f(inp["L0_attn_Wov"].T), woT=_f(inp["L0_attn_Wo"].T),
            bq=_f(inp["L0_attn_bq"]).reshape(DH, 1),
            bk=_f(inp["L0_attn_bk"]).reshape(DH, 1),
            bv=_f(inp["L0_attn_bov"]).reshape(DH, 1),
            bo=_f(inp["L0_attn_bo"]).reshape(DC, 128, 1),
            qg=qgc, iotaf=iotaf, eye128=eye128))
    res = run_bass_kernel_spmd(nca, in_maps, core_ids=list(range(NCORE)))
    x1T = np.concatenate([r["x1T"] for r in res.results], axis=1)

    w1 = _f(inp["L0_ffn_w1"])
    w2 = _f(inp["L0_ffn_w2"])
    pb = build_B()
    ncb = pb.build()
    com = dict(
        lng=_f(inp["L0_ffn_ln_g"]).reshape(DC, 128, 1),
        lnb=_f(inp["L0_ffn_ln_b"]).reshape(DC, 128, 1),
        abelW=_f(inp["L0_ffn_abel_w"]).reshape(DC, 128, DC, 128)
            .transpose(2, 1, 0, 3).copy(),
        abelB=_f(inp["L0_ffn_abel_b"]).reshape(DC, 128, 1),
        gateW=_f(inp["L0_ffn_gate_w_moe"]),
        gateBB=np.broadcast_to(_f(inp["L0_ffn_gate_b_moe"]), (128, 8)).copy(),
        w1t=w1.reshape(NEXP, DC, 128, 8, 512).transpose(0, 3, 2, 1, 4).copy(),
        b1r=_f(inp["L0_ffn_b1"]).reshape(NEXP, 32, 128).transpose(0, 2, 1).copy(),
        w2t=w2.reshape(NEXP, 8, 4, 128, DC, 128).transpose(0, 1, 3, 2, 4, 5).copy(),
        b2t=_f(inp["L0_ffn_b2"]).reshape(NEXP, DC, 128).transpose(2, 0, 1).reshape(128, NEXP * DC).copy(),
        gens=_f(inp["L0_ffn_gens"]),
        gensT=_f(inp["L0_ffn_gens"].transpose(0, 2, 1)), eye=eyeD, eye128=eye128,
        mixw=_f(inp["L0_ffn_mix_w"]), mixb=_f(inp["L0_ffn_mix_b"]).reshape(1, 1))
    in_maps = [dict(com, x1T=_f(x1T[:, c * TLOC:(c + 1) * TLOC]))
               for c in range(NCORE)]
    res = run_bass_kernel_spmd(ncb, in_maps, core_ids=list(range(NCORE)))
    x2T = np.concatenate([r["x2T"] for r in res.results], axis=1)

    win = _f(inp["L1_attn_W_in"])
    pc1 = build_C1()
    ncc1 = pc1.build()
    in_maps = []
    for c in range(NCORE):
        ch0 = c * CHL
        wxi = win[:, ch0:ch0 + CHL]
        wz = win[:, DIN + ch0:DIN + ch0 + CHL]
        in_maps.append(dict(
            x2T=x2T,
            winxi=wxi.reshape(DC, 128, 2, 128).transpose(2, 0, 1, 3).copy(),
            winz=wz.reshape(DC, 128, 2, 128).transpose(2, 0, 1, 3).copy(),
            convw=_f(inp["L1_attn_conv_w"])[ch0:ch0 + CHL].reshape(2, 128, 4),
            convb=_f(inp["L1_attn_conv_b"])[ch0:ch0 + CHL].reshape(2, 128, 1),
            wxsl=_f(inp["L1_attn_W_x"])[ch0:ch0 + CHL].reshape(2, 128, 96)))
    res = run_bass_kernel_spmd(ncc1, in_maps, core_ids=list(range(NCORE)))
    xc_all = [r["xcT"] for r in res.results]
    sz_all = [r["szT"] for r in res.results]
    xd_all = np.stack([r["xdp"] for r in res.results])

    pc2 = build_C2()
    ncc2 = pc2.build()
    selbc = np.zeros((96, 2 * DST, 128), np.float32)
    for j in range(2 * DST):
        selbc[64 + j, j, :] = 1.0
    selbc = selbc.reshape(96, 2 * DST * 128)
    in_maps = []
    wdt = _f(inp["L1_attn_W_dt"])
    wout = _f(inp["L1_attn_W_out"])
    for c in range(NCORE):
        ch0 = c * CHL
        in_maps.append(dict(
            xcT=xc_all[c], szT=sz_all[c], xdall=xd_all, selbc=selbc,
            wdtsl=wdt[:, ch0:ch0 + CHL].reshape(DTR, 2, 128).transpose(1, 0, 2).copy(),
            bdtsl=_f(inp["L1_attn_b_dt"])[ch0:ch0 + CHL].reshape(2, 128, 1),
            alogsl=_f(inp["L1_attn_A_log"])[ch0:ch0 + CHL].reshape(2, 128, DST),
            dsl=_f(inp["L1_attn_D"])[ch0:ch0 + CHL].reshape(2, 128, 1),
            woutsl=wout[ch0:ch0 + CHL].reshape(2, 128, DC, 128)
                .transpose(0, 2, 1, 3).copy()))
    res = run_bass_kernel_spmd(ncc2, in_maps, core_ids=list(range(NCORE)))
    wop_all = [r["wop"] for r in res.results]

    pd = build_D()
    ncd = pd.build()
    eg = _f(inp["L1_ffn_experts_gate"])
    eu = _f(inp["L1_ffn_experts_up"])
    ed = _f(inp["L1_ffn_experts_down"])
    sg = _f(inp["L1_ffn_shared_gate"])
    su = _f(inp["L1_ffn_shared_up"])
    sd = _f(inp["L1_ffn_shared_down"])
    com = dict(
        router=_f(inp["L1_ffn_router"]),
        egt=eg.reshape(NEXP, DC, 128, 4, 128).transpose(0, 2, 1, 3, 4).copy(),
        eut=eu.reshape(NEXP, DC, 128, 4, 128).transpose(0, 2, 1, 3, 4).copy(),
        edt=ed.reshape(NEXP, 4, 128, DC, 128).transpose(0, 3, 2, 1, 4).copy(),
        sgt=sg.reshape(DC, 128, DC, 128).transpose(2, 1, 0, 3).copy(),
        sut=su.reshape(DC, 128, DC, 128).transpose(2, 1, 0, 3).copy(),
        sdt=sd.reshape(DC, 128, DC, 128).transpose(2, 1, 0, 3).copy(),
        eye128=eye128)
    in_maps = []
    for c in range(NCORE):
        wps = np.stack([w[:, c * TLOC:(c + 1) * TLOC] for w in wop_all])
        in_maps.append(dict(com, resT=_f(x2T[:, c * TLOC:(c + 1) * TLOC]),
                            wps=_f(wps)))
    res = run_bass_kernel_spmd(ncd, in_maps, core_ids=list(range(NCORE)))
    y = np.concatenate([r["yout"] for r in res.results], axis=0)
    return y.reshape(1, T, D)
